# version 02
# baseline (speedup 1.0000x reference)
import numpy as np
import ml_dtypes
from contextlib import ExitStack
import concourse.bass as bass
import concourse.mybir as mybir
from concourse.bass_utils import run_bass_kernel_spmd

F32 = mybir.dt.float32
BF16 = mybir.dt.bfloat16
I32 = mybir.dt.int32
AL = mybir.AluOpType
AF = mybir.ActivationFunctionType
AX = mybir.AxisListType.X

B, N, HID, H, D = 2, 4096, 2048, 16, 128
TTOT = B * N
NCHUNK = 8
TCH = TTOT // NCHUNK
BLOCK = 256
KC = HID // 128
NSLC = 2048
LN_EPS = 1e-5
RMS_EPS = 1e-6
BIGM = 1.0e9

STAGE = 5


def _perm(n):
    nc2 = n // 256
    idx = np.empty(n, dtype=np.int64)
    for j2 in range(2 * nc2):
        base = 256 * (j2 // 2) + (j2 % 2)
        idx[j2 * 128:(j2 + 1) * 128] = base + 2 * np.arange(128)
    return idx


PERM = _perm(HID)


class Ctr:
    __slots__ = ("sem", "n")

    def __init__(self, sem):
        self.sem = sem
        self.n = 0

    def inc(self, instr, k=1):
        instr.then_inc(self.sem, k)
        self.n += k
        return self.n

    @property
    def tok(self):
        return (self, self.n)


def _act_set_id(nc, want):
    from concourse.hw_specs import get_activation_tables
    for idx, (name, funcs) in enumerate(get_activation_tables(nc.m.arch).items()):
        if all(f in funcs for f in want):
            return idx
    raise ValueError(f"no table set with {want}")


def build(stage):
    nc = bass.Bass("TRN2", target_bir_lowering=False, debug=False)
    SET_NLE = _act_set_id(nc, (AF.Exp, AF.Ln))
    SET_SILU = _act_set_id(nc, (AF.Silu,))
    SET_SIG = _act_set_id(nc, (AF.Sigmoid,))
    SET_GELU = _act_set_id(nc, (AF.Gelu_apprx_tanh,))

    x_ln = nc.dram_tensor("x_ln", [8, 128, HID], F32, kind="ExternalInput")
    x_res = nc.dram_tensor("x_res", [NCHUNK, 128, HID], F32, kind="ExternalInput")
    wqkv_h = nc.dram_tensor("wqkv_h", [128, KC, 768], F32, kind="ExternalInput")
    wgate_h = nc.dram_tensor("wgate_h", [128, KC, 256], F32, kind="ExternalInput")
    wout_h = nc.dram_tensor("wout_h", [128, 2, HID], F32, kind="ExternalInput")
    rmsw_h = nc.dram_tensor("rmsw_h", [128, 2], F32, kind="ExternalInput")
    slope_h = nc.dram_tensor("slope_h", [1, 2], F32, kind="ExternalInput")
    wgp_h = nc.dram_tensor("wgp_h", [1024, NSLC], I32, kind="ExternalInput")
    wup_h = nc.dram_tensor("wup_h", [1024, NSLC], I32, kind="ExternalInput")
    wdp_h = nc.dram_tensor("wdp_h", [1024, HID], I32, kind="ExternalInput")
    sg_h = nc.dram_tensor("sg_h", [1024, NSLC], F32, kind="ExternalInput")
    su_h = nc.dram_tensor("su_h", [1024, NSLC], F32, kind="ExternalInput")
    sd_h = nc.dram_tensor("sd_h", [1024, HID], F32, kind="ExternalInput")
    ident_h = nc.dram_tensor("ident_h", [128, 128], BF16, kind="ExternalInput")
    idx1_h = nc.dram_tensor("idx1_h", [128, BLOCK], F32, kind="ExternalInput")
    dml_h = nc.dram_tensor("dml_h", [128, 2, BLOCK], F32, kind="ExternalInput")
    onesf_h = nc.dram_tensor("onesf_h", [1, 128], F32, kind="ExternalInput")
    onesc_h = nc.dram_tensor("onesc_h", [128, 1], BF16, kind="ExternalInput")

    out_sh = nc.dram_tensor("out_sh", [NCHUNK, 128, HID], F32, kind="ExternalOutput")

    ag_in = [nc.dram_tensor(f"ag_in{h}", [KC, 128, TCH // 2], BF16) for h in range(2)]
    ag_out = [nc.dram_tensor(f"ag_out{h}", [NCHUNK, KC, 128, TCH // 2], BF16,
                             addr_space="Shared") for h in range(2)]
    ar_in = [nc.dram_tensor(f"ar_in{j}", [1, TCH], F32) for j in range(NCHUNK)]
    ar_out = [nc.dram_tensor(f"ar_out{j}", [1, TCH], F32) for j in range(NCHUNK)]
    wdec_d = nc.dram_tensor("wdec_d", [2, 16, KC, 128, 128], BF16)
    wddec_d = nc.dram_tensor("wddec_d", [KC, 128, HID], BF16)
    inter_d = nc.dram_tensor("inter_d", [16, 16, 128, 512], BF16)
    rs_in = [nc.dram_tensor(f"rs_in{j}", [TCH, HID], F32) for j in range(NCHUNK)]
    rs_out = [nc.dram_tensor(f"rs_out{j}", [128, HID], F32) for j in range(NCHUNK)]

    dbg = {}
    if stage == 1:
        dbg["xn_mine"] = nc.dram_tensor("dbg_xn_mine", [128, KC, TCH], BF16, kind="ExternalOutput")
        dbg["xn_c0"] = nc.dram_tensor("dbg_xn_c0", [128, KC, 512], BF16, kind="ExternalOutput")
    if stage == 2:
        dbg["qf"] = nc.dram_tensor("dbg_qf", [128, 2, TCH], BF16, kind="ExternalOutput")
        dbg["kf"] = nc.dram_tensor("dbg_kf", [128, 2, TCH], BF16, kind="ExternalOutput")
        dbg["vt"] = nc.dram_tensor("dbg_vt", [128, 8, 256], BF16, kind="ExternalOutput")
    if stage in (2, 3):
        dbg["o"] = nc.dram_tensor("dbg_o", [128, 2, TTOT], BF16, kind="ExternalOutput")
        dbg["ssq"] = nc.dram_tensor("dbg_ssq", [1, TTOT], F32, kind="ExternalOutput")
    if stage == 4:
        dbg["wdec"] = nc.dram_tensor("dbg_wdec", [2, 16, KC, 128, 128], BF16, kind="ExternalOutput")
        dbg["wddec"] = nc.dram_tensor("dbg_wddec", [KC, 128, HID], BF16, kind="ExternalOutput")
        dbg["inter"] = nc.dram_tensor("dbg_inter", [16, 16, 128, 512], BF16, kind="ExternalOutput")

    es = ExitStack()
    es123 = ExitStack()
    es4 = ExitStack()
    es5 = ExitStack()

    def S(stk, name, shape, dtype):
        return stk.enter_context(nc.sbuf_tensor(name, shape, dtype))

    with es:
        ident = S(es, "ident", [128, 128], BF16)
        idx1 = S(es, "idx1", [128, BLOCK], F32)
        dml = S(es, "dml", [128, 2, BLOCK], F32)
        onesf = S(es, "onesf", [1, 128], F32)
        onesc = S(es, "onesc", [128, 1], BF16)
        srow = S(es, "srow", [1, 2], F32)
        s_all = S(es, "s_all", [128, 2], F32)
        negs = S(es, "negs", [128, 2], F32)
        nb256 = S(es, "nb256", [128, 2], F32)
        qd = S(es, "qd", [128, 2, BLOCK], F32)
        kd = S(es, "kd", [128, 2, BLOCK], F32)
        dd = S(es, "dd", [128, 2, 2, BLOCK], F32)
        bd = S(es, "bd", [128, 2], F32)
        wout = S(es, "wout", [128, 2, HID], BF16)
        rmsw = S(es, "rmsw", [128, 2], F32)
        og_all = S(es, "og_all", [128, 2, TTOT], BF16)
        kv_f = S(es, "kv_f", [128, 2, 128], F32)
        kv_b = S(es, "kv_b", [128, 2, 128], BF16)
        ssq_ch = S(es, "ssq_ch", [1, TCH], F32)
        rvar = S(es, "rvar", [1, TCH], F32)
        rstd_j = S(es, "rstd_j", [1, TCH], F32)
        es_dec = ExitStack()
        DW = 512
        pk_t = [S(es_dec, f"pk_t{i}", [128, DW], I32) for i in range(2)]
        sct_t = [S(es_dec, f"sct_t{i}", [128, DW], F32) for i in range(2)]
        dec_u = S(es_dec, "dec_u", [128, DW], I32)
        dec_g = S(es_dec, "dec_g", [128, DW], I32)
        dec_ub = S(es_dec, "dec_ub", [128, DW], BF16)
        dec_a = S(es_dec, "dec_a", [128, DW], BF16)
        dec_b = S(es_dec, "dec_b", [128, DW], BF16)
        dec_p = S(es_dec, "dec_p", [128, DW], BF16)
        dec_w = S(es_dec, "dec_w", [128, DW], BF16)

        PB = [es.enter_context(nc.psum_tensor(f"PB{i}", [128, 1024], BF16)) for i in range(2)]
        PF = [es.enter_context(nc.psum_tensor(f"PF{i}", [128, 512], F32)) for i in range(6)]
        PF2, PF3, PF4, PF5, PF6, PF7 = PF

        DS = Ctr(es.enter_context(nc.semaphore("DS")))
        DG = Ctr(es.enter_context(nc.semaphore("DG")))
        TC = Ctr(es.enter_context(nc.semaphore("TC")))
        VC = Ctr(es.enter_context(nc.semaphore("VC")))
        SC = Ctr(es.enter_context(nc.semaphore("SC")))
        CC = Ctr(es.enter_context(nc.semaphore("CC")))

        bank_tok = {i: None for i in range(8)}

        stacks = []

        with nc.Block() as block:
            @block.sync
            def _(sync):
              try:
                T, V, Sc, G, Y = nc.tensor, nc.vector, nc.scalar, nc.gpsimd, nc.sync

                def w(eng, tok):
                    if tok is not None and tok[1] > 0:
                        eng.wait_ge(tok[0].sem, tok[1])

                def dma(q, ctr, out, in_):
                    ctr.inc(q.dma_start(out=out, in_=in_), 16)
                    return ctr.tok

                def preload(setid):
                    Sc.add_instruction(mybir.InstLoadActFuncSet(
                        name=nc.get_next_instruction_name(),
                        act_func_set_id=setid, ins=[], outs=[]))

                es_p1 = ExitStack()
                stacks.append(es_p1)
                wqkv = S(es123, "wqkv", [128, KC, 768], BF16)
                wgate = S(es123, "wgate", [128, KC, 256], BF16)
                xs = S(es_p1, "xs", [128, HID], F32)
                sqb = S(es_p1, "sqb", [128, HID], BF16)
                stat = S(es_p1, "stat", [128, 8], F32)
                xn_tm = S(es_p1, "xn_tm", [128, HID], BF16)
                xn_mine = S(es_p1, "xn_mine", [128, KC, TCH], BF16)

                preload(SET_NLE)
                t_c = None
                for dst, src in ((ident, ident_h), (idx1, idx1_h), (dml, dml_h),
                                 (onesf, onesf_h), (onesc, onesc_h), (srow, slope_h),
                                 (rmsw, rmsw_h)):
                    t_c = dma(Y, DS, dst[:], src.ap())
                t_wq = dma(G, DG, wqkv[:], wqkv_h.ap())
                t_wg = dma(G, DG, wgate[:], wgate_h.ap())
                t_wo = dma(G, DG, wout[:], wout_h.ap())

                w(T, t_c)
                TC.inc(T.matmul(PF2[:, 0:2], onesf[:], srow[:], start=True, stop=True))
                t = TC.tok
                w(V, t)
                VC.inc(V.tensor_copy(s_all[:], PF2[:, 0:2]))
                w(V, VC.tok)
                VC.inc(V.tensor_scalar(negs[:], s_all[:], -1.0, None, AL.mult))
                VC.inc(V.tensor_scalar(nb256[:], s_all[:], -256.0, None, AL.mult))
                VC.inc(V.memset(kv_f[:], 0.0))
                t_setup_v = VC.tok
                w(Sc, t_setup_v)
                for hh in range(2):
                    SC.inc(Sc.activation(qd[:, hh, :], idx1[:], AF.Exp, scale=negs[:, hh:hh + 1]))
                    SC.inc(Sc.activation(kd[:, hh, :], idx1[:], AF.Exp,
                                         scale=s_all[:, hh:hh + 1], bias=nb256[:, hh:hh + 1]))
                    SC.inc(Sc.activation(dd[:, hh, :, :], dml[:], AF.Exp, scale=negs[:, hh:hh + 1]))
                    SC.inc(Sc.activation(bd[:, hh:hh + 1], s_all[:, hh:hh + 1], AF.Exp, scale=-256.0))
                w(Sc, t_setup_v)
                SC.inc(Sc.copy(kv_b[:], kv_f[:]))
                t_decay = SC.tok
                bank_tok[2] = t_setup_v

                t_xs_v = None
                t_xs_s = None
                t_ag2 = [None, None]
                for u in range(8):
                    w(Y, t_xs_v)
                    w(Y, t_xs_s)
                    t_x = dma(Y, DS, xs[:], x_ln.ap()[u])
                    w(V, t_x)
                    VC.inc(V.reduce_sum(stat[:, 0:1], xs[:], axis=AX))
                    t_sum = VC.tok
                    w(Sc, t_x)
                    SC.inc(Sc.activation(sqb[:], xs[:], AF.Square, accum_out=stat[:, 1:2]))
                    t_ssq = SC.tok
                    t_xs_s = t_ssq
                    w(V, t_sum)
                    VC.inc(V.tensor_scalar(stat[:, 2:3], stat[:, 0:1], 1.0 / HID, None, AL.mult))
                    w(V, VC.tok)
                    VC.inc(V.tensor_tensor(stat[:, 3:4], stat[:, 2:3], stat[:, 2:3], AL.mult))
                    w(V, t_ssq)
                    VC.inc(V.tensor_scalar(stat[:, 4:5], stat[:, 1:2], 1.0 / HID, LN_EPS, AL.mult, AL.add))
                    w(V, VC.tok)
                    VC.inc(V.tensor_tensor(stat[:, 4:5], stat[:, 4:5], stat[:, 3:4], AL.subtract))
                    t_var = VC.tok
                    w(Sc, t_var)
                    SC.inc(Sc.activation(stat[:, 5:6], stat[:, 4:5], AF.Ln))
                    w(Sc, SC.tok)
                    SC.inc(Sc.activation(stat[:, 6:7], stat[:, 5:6], AF.Exp, scale=-0.5))
                    t_rstd = SC.tok
                    w(V, t_rstd)
                    VC.inc(V.tensor_scalar(xn_tm[:], xs[:], stat[:, 2:3], stat[:, 6:7],
                                           AL.subtract, AL.mult))
                    t_norm = VC.tok
                    t_xs_v = t_norm
                    w(T, t_norm)
                    for q in range(KC):
                        b = q % 2
                        w(T, bank_tok[b])
                        base = 256 * (q // 2) + (q % 2)
                        TC.inc(T.transpose(PB[b][:, 0:128], xn_tm[:, base:base + 255 + 1 - (q % 2):2], ident[:]))
                        w(Sc, TC.tok)
                        SC.inc(Sc.copy(xn_mine[:, q, 128 * u:128 * u + 128], PB[b][:, 0:128]))
                        bank_tok[b] = SC.tok
                    if u == 3 or u == 7:
                        hf_ = u // 4
                        w(G, SC.tok)
                        t_agst = dma(G, DG, ag_in[hf_].ap().rearrange("q p t -> p q t"),
                                     xn_mine[:, :, 512 * hf_:512 * hf_ + 512])
                        w(G, t_agst)
                        CC.inc(G.collective_compute(
                            "AllGather", AL.bypass, replica_groups=[list(range(8))],
                            ins=[ag_in[hf_].ap().opt()], outs=[ag_out[hf_].ap().opt()]))
                        t_ag2[hf_] = CC.tok
                t_ag = t_ag2[1]

                if stage == 1:
                    dma(G, DG, dbg["xn_mine"].ap(), xn_mine[:])
                es_p1.close()
                xn_ch = [S(es123, f"xn_ch{i}", [128, KC, 512], BF16) for i in range(2)]
                q_fm = S(es123, "q_fm", [128, 2, TCH], BF16)
                k_fm = S(es123, "k_fm", [128, 2, TCH], BF16)
                v_tm = S(es123, "v_tm", [128, 8, 256], BF16)
                g_ch = [S(es123, f"g_ch{i}", [128, 2, TCH], BF16) for i in range(2)]
                qt_s = S(es123, "qt_s", [128, BLOCK], BF16)
                kt_s = S(es123, "kt_s", [128, BLOCK], BF16)
                ktT = S(es123, "ktT", [128, 2, 128], BF16)
                sc_sb = S(es123, "sc_sb", [128, 2, BLOCK], BF16)
                o2_s = S(es123, "o2_s", [128, BLOCK], BF16)
                if stage == 1:
                    w(Y, t_ag)
                    t_l = dma(Y, DS, xn_ch[0][:],
                              ag_out[0].ap()[0].rearrange("q p t -> p q t"))
                    w(G, t_l)
                    dma(G, DG, dbg["xn_c0"].ap(), xn_ch[0][:])
                    w(G, DG.tok); w(Y, DS.tok); w(Sc, SC.tok); w(V, VC.tok); w(T, TC.tok)
                    return

                jobs = [(m, kk, qt) for m in range(2) for kk in range(8) for qt in range(4)]
                jobs += [(2, kk, qt) for kk in range(8) for qt in range(4)]
                N_GU_JOBS = 64
                ji = [0]
                pslot = [0]
                slot_tok = [None, None]
                dec_w_store = [None]

                def emit_decode_one():
                    if ji[0] >= len(jobs):
                        return
                    mat, kk, qt = jobs[ji[0]]
                    ji[0] += 1
                    sl = pslot[0]
                    pslot[0] ^= 1
                    pk, sct = pk_t[sl], sct_t[sl]
                    src_p = (wgp_h, wup_h, wdp_h)[mat]
                    src_s = (sg_h, su_h, sd_h)[mat]
                    W = DW
                    w(Y, slot_tok[sl])
                    dma(Y, DS, pk[:, :], src_p.ap()[128 * kk:128 * kk + 128, W * qt:W * qt + W])
                    t_pk = dma(Y, DS, sct[:, :], src_s.ap()[128 * kk:128 * kk + 128, W * qt:W * qt + W])
                    w(V, t_pk)
                    for delta in range(2):
                        if delta == 0:
                            VC.inc(V.tensor_single_scalar(dec_u[:, :], pk[:, :], 7, AL.bitwise_and))
                            VC.inc(V.tensor_scalar(dec_g[:, :], pk[:, :], 3, 1,
                                                   AL.logical_shift_right, AL.bitwise_and))
                        else:
                            w(V, VC.tok)
                            VC.inc(V.tensor_scalar(dec_u[:, :], pk[:, :], 4, 7,
                                                   AL.logical_shift_right, AL.bitwise_and))
                            VC.inc(V.tensor_scalar(dec_g[:, :], pk[:, :], 7, 1,
                                                   AL.logical_shift_right, AL.bitwise_and))
                        w(V, VC.tok)
                        VC.inc(V.tensor_copy(dec_ub[:, :], dec_u[:, :]))
                        VC.inc(V.tensor_scalar(dec_p[:, :], dec_g[:, :], -2.0, 1.0, AL.mult, AL.add))
                        w(V, VC.tok)
                        VC.inc(V.tensor_scalar(dec_a[:, :], dec_ub[:, :], 2.0, -8.0, AL.mult, AL.add))
                        w(V, VC.tok)
                        VC.inc(V.scalar_tensor_tensor(dec_b[:, :], dec_ub[:, :], -2.0, dec_a[:, :],
                                                      AL.add, AL.max))
                        w(V, VC.tok)
                        VC.inc(V.scalar_tensor_tensor(dec_a[:, :], dec_ub[:, :], 0.5, dec_b[:, :],
                                                      AL.mult, AL.max))
                        w(V, VC.tok)
                        VC.inc(V.tensor_tensor(dec_b[:, :], dec_a[:, :], dec_p[:, :], AL.mult))
                        w(V, VC.tok)
                        w(V, dec_w_store[0])
                        VC.inc(V.tensor_tensor(dec_w[:, :], dec_b[:, :], sct[:, :], AL.mult))
                        t_w = VC.tok
                        w(G, t_w)
                        q = 2 * kk + delta
                        if mat < 2:
                            dec_w_store[0] = dma(G, DG,
                                wdec_d.ap()[mat, 4 * qt:4 * qt + 4, q].rearrange("nt p n -> p nt n"),
                                dec_w[:, :])
                        else:
                            dec_w_store[0] = dma(G, DG, wddec_d.ap()[q, :, W * qt:W * qt + W], dec_w[:, :])
                    slot_tok[sl] = VC.tok

                xn_cons = [None, None]
                g_cons = [None, None]
                t_attn_prev = None
                t_kvb = t_decay
                t_qt_cons = None
                t_kt_cons = None
                t_ktT_cons = None
                t_scsb_cons = None
                t_sq_mm = None
                ar_tok = [None] * NCHUNK
                ssq_dma_tok = [None]
                t_silu_last = None

                gate_guard = {"ln": None, "bc": None}

                def emit_gating(jm):
                    gsl = g_ch[jm % 2]
                    w(Y, ar_tok[jm])
                    w(Y, gate_guard["ln"])
                    t_aro = dma(Y, DS, rvar[:], ar_out[jm].ap())
                    preload(SET_NLE)
                    w(V, t_aro)
                    VC.inc(V.tensor_scalar(rvar[:], rvar[:], 1.0 / HID, RMS_EPS, AL.mult, AL.add))
                    w(Sc, VC.tok)
                    w(Sc, gate_guard["bc"])
                    SC.inc(Sc.activation(rstd_j[:], rvar[:], AF.Ln))
                    gate_guard["ln"] = SC.tok
                    w(Sc, SC.tok)
                    SC.inc(Sc.activation(rstd_j[:], rstd_j[:], AF.Exp, scale=-0.5))
                    t_rst = SC.tok
                    for th in range(2):
                        bk = 2 + th
                        w(T, t_rst)
                        w(T, bank_tok[bk])
                        TC.inc(T.matmul(PF[th][:, :], onesf[:], rstd_j[0:1, 512 * th:512 * th + 512],
                                        start=True, stop=True))
                        t_bc = TC.tok
                        gate_guard["bc"] = t_bc
                        w(V, t_bc)
                        for hh in range(2):
                            sli = slice(TCH * jm + 512 * th, TCH * jm + 512 * th + 512)
                            VC.inc(V.tensor_tensor(og_all[:, hh, sli], og_all[:, hh, sli],
                                                   gsl[:, hh, 512 * th:512 * th + 512], AL.mult))
                            w(V, VC.tok)
                            VC.inc(V.scalar_tensor_tensor(og_all[:, hh, sli], og_all[:, hh, sli],
                                                          rmsw[:, hh:hh + 1], PF[th][:, :],
                                                          AL.mult, AL.mult))
                        bank_tok[bk] = VC.tok
                    g_cons[jm % 2] = VC.tok

                for j in range(NCHUNK):
                    t_xl = [None, None]
                    for th in range(2):
                        w(Y, t_ag2[th])
                        w(Y, xn_cons[th])
                        t_xl[th] = dma(Y, DS, xn_ch[th][:],
                                       ag_out[th].ap()[j].rearrange("q p t -> p q t"))
                    if j == 0:
                        T.wait_ge(DG.sem, t_wo[1])
                    preload(SET_SILU)
                    for cc in range(4):
                        for th in range(2):
                            bk = 2 + cc % 2
                            pf = PF[cc % 2]
                            w(T, t_xl[th])
                            w(T, bank_tok[bk])
                            for q in range(KC):
                                mmi = T.matmul(pf[:, :], wqkv[:, q, 128 * cc:128 * cc + 128],
                                               xn_ch[th][:, q, :], start=(q == 0), stop=(q == KC - 1))
                                if q == KC - 1:
                                    TC.inc(mmi)
                            t_mm = TC.tok
                            xn_cons[th] = t_mm
                            hh, isq = cc % 2, cc < 2
                            w(Sc, t_mm)
                            w(Sc, t_attn_prev)
                            SC.inc(Sc.activation((q_fm if isq else k_fm)[:, hh, 512 * th:512 * th + 512],
                                                 pf[:, :], AF.Silu))
                            bank_tok[bk] = SC.tok
                    for u in range(8):
                        th = u // 4
                        bk = 4 + u % 2
                        pf = PF[2 + u % 2]
                        w(T, t_xl[th])
                        w(T, bank_tok[bk])
                        for q in range(KC):
                            mmi = T.matmul(pf[:, 0:256], xn_ch[th][:, q, 128 * (u % 4):128 * (u % 4) + 128],
                                           wqkv[:, q, 512:768], start=(q == 0), stop=(q == KC - 1))
                            if q == KC - 1:
                                TC.inc(mmi)
                        t_mm = TC.tok
                        xn_cons[th] = t_mm
                        w(Sc, t_mm)
                        w(Sc, t_attn_prev)
                        SC.inc(Sc.activation(v_tm[:, u, :], pf[:, 0:256], AF.Silu))
                        bank_tok[bk] = SC.tok
                    t_silu_last = SC.tok
                    preload(SET_SIG)
                    gsl = g_ch[j % 2]
                    for hh in range(2):
                        for th in range(2):
                            w(T, t_xl[th])
                            w(T, bank_tok[7])
                            for q in range(KC):
                                mmi = T.matmul(PF7[:, :], wgate[:, q, 128 * hh:128 * hh + 128],
                                               xn_ch[th][:, q, :], start=(q == 0), stop=(q == KC - 1))
                                if q == KC - 1:
                                    TC.inc(mmi)
                            t_mm = TC.tok
                            xn_cons[th] = t_mm
                            w(Sc, t_mm)
                            w(Sc, g_cons[j % 2])
                            SC.inc(Sc.activation(gsl[:, hh, 512 * th:512 * th + 512], PF7[:, :], AF.Sigmoid))
                            bank_tok[7] = SC.tok
                    if stage == 2 and j == 0:
                        w(G, t_silu_last)
                        dma(G, DG, dbg["qf"].ap(), q_fm[:])
                        dma(G, DG, dbg["kf"].ap(), k_fm[:])
                        dma(G, DG, dbg["vt"].ap(), v_tm[:])

                    if j == 4:
                        w(V, t_attn_prev)
                        VC.inc(V.memset(kv_f[:], 0.0))
                        w(Sc, VC.tok)
                        SC.inc(Sc.copy(kv_b[:], kv_f[:]))
                        t_kvb = SC.tok
                    for blk in range(4):
                        l0 = BLOCK * blk
                        for hh in range(2):
                            gl = TCH * j + l0
                            w(V, t_silu_last)
                            w(V, t_qt_cons)
                            w(V, t_kt_cons)
                            VC.inc(V.tensor_tensor(qt_s[:], q_fm[:, hh, l0:l0 + BLOCK], qd[:, hh, :], AL.mult))
                            VC.inc(V.tensor_tensor(kt_s[:], k_fm[:, hh, l0:l0 + BLOCK], kd[:, hh, :], AL.mult))
                            t_qtkt = VC.tok
                            for mh in range(2):
                                w(T, bank_tok[2 + mh])
                                TC.inc(T.matmul(PF[mh][:, 0:256],
                                                k_fm[:, hh, l0 + 128 * mh:l0 + 128 * mh + 128],
                                                q_fm[:, hh, l0:l0 + BLOCK], start=True, stop=True))
                            t_sc_mm = TC.tok
                            w(V, t_sc_mm)
                            w(V, t_scsb_cons)
                            for mh in range(2):
                                VC.inc(V.tensor_tensor(sc_sb[:, mh, :], PF[mh][:, 0:256],
                                                       dd[:, hh, mh, :], AL.mult))
                            t_scm = VC.tok
                            bank_tok[2] = t_scm
                            bank_tok[3] = t_scm
                            w(T, t_kvb)
                            w(T, t_qtkt)
                            w(T, bank_tok[4])
                            TC.inc(T.matmul(PF4[:, 0:256], kv_b[:, hh, :], qt_s[:], start=True, stop=False))
                            t_qt_cons = TC.tok
                            w(T, t_scm)
                            for mh in range(2):
                                TC.inc(T.matmul(PF4[:, 0:256], v_tm[:, 2 * blk + mh, 128 * hh:128 * hh + 128],
                                                sc_sb[:, mh, :], start=False, stop=(mh == 1)))
                            t_o_mm = TC.tok
                            t_scsb_cons = t_o_mm
                            for mh in range(2):
                                w(T, bank_tok[mh])
                                TC.inc(T.transpose(PB[mh][:, 0:128], kt_s[:, 128 * mh:128 * mh + 128], ident[:]))
                                t_tr = TC.tok
                                if mh == 1:
                                    t_kt_cons = t_tr
                                w(Sc, t_tr)
                                w(Sc, t_ktT_cons)
                                SC.inc(Sc.copy(ktT[:, mh, :], PB[mh][:, 0:128]))
                                bank_tok[mh] = SC.tok
                            t_ktT = SC.tok
                            w(T, t_ktT)
                            w(T, bank_tok[5])
                            for mh in range(2):
                                TC.inc(T.matmul(PF5[:, 0:128], ktT[:, mh, :],
                                                v_tm[:, 2 * blk + mh, 128 * hh:128 * hh + 128],
                                                start=(mh == 0), stop=(mh == 1)))
                            t_a_mm = TC.tok
                            t_ktT_cons = t_a_mm
                            w(Sc, t_o_mm)
                            SC.inc(Sc.copy(og_all[:, hh, gl:gl + BLOCK], PF4[:, 0:256]))
                            t_oev = SC.tok
                            bank_tok[4] = t_oev
                            w(V, t_oev)
                            w(V, t_sq_mm)
                            VC.inc(V.tensor_tensor(o2_s[:], og_all[:, hh, gl:gl + BLOCK],
                                                   og_all[:, hh, gl:gl + BLOCK], AL.mult))
                            t_o2 = VC.tok
                            w(T, t_o2)
                            psq = (PF6, PF7)[blk // 2]
                            if blk == 0 and hh == 0:
                                w(T, bank_tok[6])
                            if blk == 2 and hh == 0:
                                w(T, bank_tok[7])
                            TC.inc(T.matmul(psq[0:1, 256 * (blk % 2):256 * (blk % 2) + 256],
                                            onesc[:], o2_s[:], start=(hh == 0), stop=(hh == 1)))
                            t_sq_mm = TC.tok
                            w(V, t_a_mm)
                            w(V, t_kvb)
                            VC.inc(V.scalar_tensor_tensor(kv_f[:, hh, :], kv_f[:, hh, :], bd[:, hh:hh + 1],
                                                          PF5[:, 0:128], AL.mult, AL.add))
                            t_kvf = VC.tok
                            bank_tok[5] = t_kvf
                            w(Sc, t_kvf)
                            SC.inc(Sc.copy(kv_b[:, hh, :], kv_f[:, hh, :]))
                            t_kvb = SC.tok
                    t_attn_prev = t_sq_mm
                    w(V, t_sq_mm)
                    w(V, ssq_dma_tok[0])
                    VC.inc(V.tensor_copy(ssq_ch[0:1, 0:512], PF6[0:1, :]))
                    VC.inc(V.tensor_copy(ssq_ch[0:1, 512:1024], PF7[0:1, :]))
                    t_sev = VC.tok
                    bank_tok[6] = t_sev
                    bank_tok[7] = t_sev
                    w(G, t_sev)
                    t_ast = dma(G, DG, ar_in[j].ap(), ssq_ch[0:1, :])
                    ssq_dma_tok[0] = t_ast
                    w(G, t_ast)
                    CC.inc(G.collective_compute(
                        "AllReduce", AL.add, replica_groups=[list(range(8))],
                        ins=[ar_in[j].ap().opt()], outs=[ar_out[j].ap().opt()]))
                    ar_tok[j] = CC.tok
                    if j > 0 and stage != 2:
                        emit_gating(j - 1)
                    for _ in range(8):
                        emit_decode_one()
                if stage != 2:
                    emit_gating(NCHUNK - 1)

                if stage in (2, 3):
                    w(G, VC.tok)
                    w(G, SC.tok)
                    dma(G, DG, dbg["o"].ap(), og_all[:])
                    for jq in range(NCHUNK):
                        dma(G, DG, dbg["ssq"].ap()[0:1, TCH * jq:TCH * jq + TCH], ar_in[jq].ap())
                    w(G, DG.tok); w(Y, DS.tok); w(Sc, SC.tok); w(V, VC.tok); w(T, TC.tok)
                    return

                while ji[0] < N_GU_JOBS:
                    emit_decode_one()
                t_gu_dec = DG.tok

                es123.close()
                xn_tc = [S(es4, f"xn_tc{i}", [128, KC, 512], BF16) for i in range(2)]
                wl_g = [S(es4, f"wl_g{i}", [128, KC, 4, 128], BF16) for i in range(2)]
                wl_u = [S(es4, f"wl_u{i}", [128, KC, 4, 128], BF16) for i in range(2)]
                gel_s = S(es4, "gel_s", [128, 4, 512], BF16)
                up_s = S(es4, "up_s", [128, 512], BF16)
                int_s = [S(es4, f"int_s{i}", [128, 4, 512], BF16) for i in range(2)]

                t_bar = (TC.tok, VC.tok, SC.tok)
                for tok in t_bar:
                    w(Y, tok)

                preload(SET_GELU)
                xtc_cons = [None, None]
                wl_cons = [None, None]
                ist_tok = [None, None]
                gel_cons = [None] * 4
                for ntg in range(4):
                    ws = ntg % 2
                    w(Y, wl_cons[ws])
                    if ntg == 0:
                        w(Y, t_gu_dec)
                    for nt4 in range(4):
                        t_wgl = dma(Y, DS, wl_g[ws][:, :, nt4, :],
                                    wdec_d.ap()[0, 4 * ntg + nt4].rearrange("q p n -> p q n"))
                        t_wul = dma(Y, DS, wl_u[ws][:, :, nt4, :],
                                    wdec_d.ap()[1, 4 * ntg + nt4].rearrange("q p n -> p q n"))
                    for tc in range(16):
                        ts_ = tc % 2
                        jj, th = tc // 2, tc % 2
                        w(Y, xtc_cons[ts_])
                        t_x = dma(Y, DS, xn_tc[ts_][:],
                                  ag_out[th].ap()[jj].rearrange("q p t -> p q t"))
                        isl = tc % 2
                        w(T, t_wul)
                        w(T, t_x)
                        for nt4 in range(4):
                            bk = 4 + (nt4 % 2) * 2
                            pf = PF[2 + (nt4 % 2) * 2]
                            w(T, bank_tok[bk])
                            for q in range(KC):
                                mmi = T.matmul(pf[:, :], wl_g[ws][:, q, nt4, :],
                                               xn_tc[ts_][:, q, :], start=(q == 0), stop=(q == KC - 1))
                                if q == KC - 1:
                                    TC.inc(mmi)
                            t_mm = TC.tok
                            w(Sc, t_mm)
                            w(Sc, gel_cons[nt4])
                            SC.inc(Sc.activation(gel_s[:, nt4, :], pf[:, :], AF.Gelu_apprx_tanh))
                            bank_tok[bk] = SC.tok
                            bku = 5 + (nt4 % 2) * 2
                            pfu = PF[3 + (nt4 % 2) * 2]
                            w(T, bank_tok[bku])
                            for q in range(KC):
                                mmi = T.matmul(pfu[:, :], wl_u[ws][:, q, nt4, :],
                                               xn_tc[ts_][:, q, :], start=(q == 0), stop=(q == KC - 1))
                                if q == KC - 1:
                                    TC.inc(mmi)
                            t_mmu = TC.tok
                            xtc_cons[ts_] = t_mmu
                            wl_cons[ws] = t_mmu
                            w(V, t_mmu)
                            w(V, ist_tok[isl])
                            VC.inc(V.tensor_copy(up_s[:, :], pfu[:, :]))
                            w(V, VC.tok)
                            w(V, bank_tok[bk])
                            VC.inc(V.tensor_tensor(int_s[isl][:, nt4, :], gel_s[:, nt4, :], up_s[:, :], AL.mult))
                            bank_tok[bku] = VC.tok
                            gel_cons[nt4] = VC.tok
                        t_int = VC.tok
                        w(G, t_int)
                        ist_tok[isl] = dma(G, DG,
                                           inter_d.ap()[4 * ntg:4 * ntg + 4, tc].rearrange("nt p t -> p nt t"),
                                           int_s[isl][:, :, :])
                        if tc % 2 == 0:
                            emit_decode_one()
                while ji[0] < len(jobs):
                    emit_decode_one()

                if stage == 4:
                    w(G, DG.tok)
                    dma(G, DG, dbg["wdec"].ap(), wdec_d.ap())
                    dma(G, DG, dbg["wddec"].ap(), wddec_d.ap())
                    dma(G, DG, dbg["inter"].ap(), inter_d.ap())
                    w(G, DG.tok); w(Y, DS.tok); w(Sc, SC.tok); w(V, VC.tok); w(T, TC.tok)
                    return

                es4.close()
                es_dec.close()
                wd_res = S(es5, "wd_res", [128, KC, HID], BF16)
                int_l = [S(es5, f"int_l{i}", [128, KC, 512], BF16) for i in range(2)]
                rs_st = [S(es5, f"rs_st{i}", [128, HID], F32) for i in range(2)]
                xr_s = S(es5, "xr_s", [128, HID], F32)
                ro_s = S(es5, "ro_s", [128, HID], F32)
                ou_s = S(es5, "ou_s", [128, HID], F32)

                t_bar2 = (TC.tok, VC.tok, SC.tok)
                for tok in t_bar2:
                    w(Y, tok)
                w(Y, DG.tok)
                t_wd = dma(Y, DS, wd_res[:], wddec_d.ap().rearrange("q p h -> p q h"))

                il_cons = [None, None]
                rs_dma = [None, None]
                rs_tok = [None] * NCHUNK
                p6_state = {"add": None, "ost": None}

                def p6(jj):
                    w(Y, rs_tok[jj])
                    w(Y, p6_state["add"])
                    t_ro = dma(Y, DS, ro_s[:], rs_out[jj].ap())
                    t_xr = dma(Y, DS, xr_s[:], x_res.ap()[jj])
                    w(V, t_xr)
                    w(V, p6_state["ost"])
                    VC.inc(V.tensor_tensor(ou_s[:], ro_s[:], xr_s[:], AL.add))
                    p6_state["add"] = VC.tok
                    w(G, p6_state["add"])
                    p6_state["ost"] = dma(G, DG, out_sh.ap()[jj], ou_s[:])

                for tc in range(16):
                    ts_ = tc % 2
                    w(Y, il_cons[ts_])
                    t_il = dma(Y, DS, int_l[ts_][:], inter_d.ap()[:, tc].rearrange("nt p t -> p nt t"))
                    for s in range(4):
                        tt = 4 * tc + s
                        rsl = tt % 2
                        w(T, t_il)
                        w(T, t_wd)
                        for bk in range(4, 8):
                            w(T, bank_tok[bk])
                        for q in range(KC):
                            for h4 in range(4):
                                T.matmul(PF[2 + h4][:, :], int_l[ts_][:, q, 128 * s:128 * s + 128],
                                         wd_res[:, q, 512 * h4:512 * h4 + 512],
                                         start=(q == 0), stop=False)
                        for jc in range(2):
                            for h4 in range(4):
                                mmi = T.matmul(PF[2 + h4][:, :], og_all[:, jc, 128 * tt:128 * tt + 128],
                                               wout[:, jc, 512 * h4:512 * h4 + 512],
                                               start=False, stop=(jc == 1))
                                if jc == 1:
                                    TC.inc(mmi)
                        t_mm = TC.tok
                        il_cons[ts_] = t_mm
                        w(Sc, t_mm)
                        w(Sc, rs_dma[rsl])
                        SC.inc(Sc.copy(rs_st[rsl][:, 0:512], PF4[:, :]))
                        SC.inc(Sc.copy(rs_st[rsl][:, 512:1024], PF5[:, :]))
                        bank_tok[4] = SC.tok
                        bank_tok[5] = SC.tok
                        t_ev_s = SC.tok
                        w(V, t_mm)
                        w(V, rs_dma[rsl])
                        VC.inc(V.tensor_copy(rs_st[rsl][:, 1024:1536], PF6[:, :]))
                        VC.inc(V.tensor_copy(rs_st[rsl][:, 1536:2048], PF7[:, :]))
                        bank_tok[6] = VC.tok
                        bank_tok[7] = VC.tok
                        t_ev_v = VC.tok
                        w(G, t_ev_s)
                        w(G, t_ev_v)
                        jj, row = tt // 8, tt % 8
                        rs_dma[rsl] = dma(G, DG, rs_in[jj].ap()[128 * row:128 * row + 128, :],
                                          rs_st[rsl][:, :])
                    if tc % 2 == 1:
                        jj = tc // 2
                        w(G, rs_dma[0])
                        w(G, rs_dma[1])
                        CC.inc(G.collective_compute(
                            "ReduceScatter", AL.add, replica_groups=[list(range(8))],
                            ins=[rs_in[jj].ap().opt()], outs=[rs_out[jj].ap().opt()]))
                        rs_tok[jj] = CC.tok
                        if jj >= 1:
                            p6(jj - 1)
                p6(NCHUNK - 1)
                w(G, DG.tok)
                w(Y, DS.tok)
                w(Sc, SC.tok)
                w(V, VC.tok)
                w(T, TC.tok)
              finally:
                for st in stacks + [es5, es4, es123, es_dec]:
                    try:
                        st.close()
                    except Exception:
                        pass
    return nc, dbg


def _host_prep(inputs):
    x = np.ascontiguousarray(np.asarray(inputs["L0_attn_hidden_states"], dtype=np.float32).reshape(TTOT, HID))
    slope = np.asarray(inputs["L0_attn_slope_rate"], dtype=np.float32).reshape(H)
    w_qkv = np.asarray(inputs["L0_attn_w_qkv"], dtype=np.float32)
    w_gate = np.asarray(inputs["L0_attn_w_gate"], dtype=np.float32)
    w_out = np.asarray(inputs["L0_attn_w_out"], dtype=np.float32)
    rms_w = np.asarray(inputs["L0_attn_rms_weight"], dtype=np.float32)
    wgp = np.asarray(inputs["L0_ffn_gate_W_packed"], dtype=np.int32)
    sgs = np.asarray(inputs["L0_ffn_gate_scales"], dtype=np.float32)
    wup = np.asarray(inputs["L0_ffn_up_W_packed"], dtype=np.int32)
    sus = np.asarray(inputs["L0_ffn_up_scales"], dtype=np.float32)
    wdp = np.asarray(inputs["L0_ffn_down_W_packed"], dtype=np.int32)
    sds = np.asarray(inputs["L0_ffn_down_scales"], dtype=np.float32)

    wqkv_p = w_qkv[PERM]
    wgate_p = w_gate[PERM]
    wgpT = np.ascontiguousarray(wgp.T)
    wupT = np.ascontiguousarray(wup.T)
    wdpT = np.ascontiguousarray(wdp.T)
    sgsT = np.ascontiguousarray(sgs.T)
    susT = np.ascontiguousarray(sus.T)
    sdsT = np.ascontiguousarray(sds.T)

    ident = np.eye(128, dtype=ml_dtypes.bfloat16)
    idx1 = np.ascontiguousarray(np.tile(np.arange(1, BLOCK + 1, dtype=np.float32), (128, 1)))
    dmlt = np.empty((128, 2, BLOCK), dtype=np.float32)
    for qq in range(2):
        m = 128 * qq + np.arange(128)[:, None]
        l = np.arange(BLOCK)[None, :]
        dmlt[:, qq, :] = np.where(l >= m, (l - m).astype(np.float32), BIGM)
    onesf = np.ones((1, 128), dtype=np.float32)
    onesc = np.ones((128, 1), dtype=ml_dtypes.bfloat16)

    maps = []
    xr4 = x.reshape(NCHUNK, 8, 128, HID)
    for c in range(8):
        cols = np.r_[256 * c:256 * c + 256, 2048 + 256 * c:2048 + 256 * c + 256,
                     4096 + 256 * c:4096 + 256 * c + 256]
        nsl = 2048 * c + PERM
        m = dict(
            x_ln=np.ascontiguousarray(x[1024 * c:1024 * c + 1024].reshape(8, 128, HID)),
            x_res=np.ascontiguousarray(xr4[:, c]),
            wqkv_h=np.ascontiguousarray(wqkv_p[:, cols].reshape(KC, 128, 768).transpose(1, 0, 2)),
            wgate_h=np.ascontiguousarray(
                wgate_p[:, 256 * c:256 * c + 256].reshape(KC, 128, 256).transpose(1, 0, 2)),
            wout_h=np.ascontiguousarray(w_out[256 * c:256 * c + 256].reshape(2, 128, HID).transpose(1, 0, 2)),
            rmsw_h=np.ascontiguousarray(rms_w[256 * c:256 * c + 256].reshape(2, 128).T),
            slope_h=np.ascontiguousarray(slope[[2 * c, 2 * c + 1]].reshape(1, 2)),
            wgp_h=np.ascontiguousarray(wgpT[:, nsl]),
            wup_h=np.ascontiguousarray(wupT[:, nsl]),
            wdp_h=np.ascontiguousarray(wdpT[1024 * c:1024 * c + 1024]),
            sg_h=np.ascontiguousarray(np.repeat(sgsT[:, nsl], 16, axis=0)),
            su_h=np.ascontiguousarray(np.repeat(susT[:, nsl], 16, axis=0)),
            sd_h=np.ascontiguousarray(np.repeat(sdsT[64 * c:64 * c + 64], 16, axis=0)),
            ident_h=ident, idx1_h=idx1, dml_h=dmlt, onesf_h=onesf, onesc_h=onesc,
        )
        maps.append(m)
    return maps


def kernel(**inputs):
    maps = _host_prep(inputs)
    nc, dbg = build(STAGE)
    res = run_bass_kernel_spmd(nc, maps, core_ids=list(range(8)))
    if STAGE < 5:
        return res
    out = np.empty((TTOT, HID), dtype=np.float32)
    for c in range(8):
        sh = res.results[c]["out_sh"]
        for j in range(NCHUNK):
            out[1024 * j + 128 * c:1024 * j + 128 * c + 128] = sh[j]
    return out.reshape(B, N, HID)


# version 05
# speedup vs baseline: 1.0517x; 1.0517x over previous
import numpy as np
import ml_dtypes
from contextlib import ExitStack
import concourse.bass as bass
import concourse.mybir as mybir
from concourse.bass_utils import run_bass_kernel_spmd

F32 = mybir.dt.float32
BF16 = mybir.dt.bfloat16
I32 = mybir.dt.int32
AL = mybir.AluOpType
AF = mybir.ActivationFunctionType
AX = mybir.AxisListType.X

B, N, HID, H, D = 2, 4096, 2048, 16, 128
TTOT = B * N
NCHUNK = 8
TCH = TTOT // NCHUNK
BLOCK = 256
KC = HID // 128
NSLC = 2048
LN_EPS = 1e-5
RMS_EPS = 1e-6
BIGM = 1.0e9

STAGE = 5
INTERLEAVE_FLAG = [True]


def _perm(n):
    nc2 = n // 256
    idx = np.empty(n, dtype=np.int64)
    for j2 in range(2 * nc2):
        base = 256 * (j2 // 2) + (j2 % 2)
        idx[j2 * 128:(j2 + 1) * 128] = base + 2 * np.arange(128)
    return idx


PERM = _perm(HID)


class Ctr:
    __slots__ = ("sem", "n")

    def __init__(self, sem):
        self.sem = sem
        self.n = 0

    def inc(self, instr, k=1):
        instr.then_inc(self.sem, k)
        self.n += k
        return self.n

    @property
    def tok(self):
        return (self, self.n)


def _act_set_id(nc, want):
    from concourse.hw_specs import get_activation_tables
    for idx, (name, funcs) in enumerate(get_activation_tables(nc.m.arch).items()):
        if all(f in funcs for f in want):
            return idx
    raise ValueError(f"no table set with {want}")


def build(stage, interleave=None):
    INTERLEAVE = INTERLEAVE_FLAG[0]
    nc = bass.Bass("TRN2", target_bir_lowering=False, debug=False)
    SET_NLE = _act_set_id(nc, (AF.Exp, AF.Ln))
    SET_SILU = _act_set_id(nc, (AF.Silu,))
    SET_SIG = _act_set_id(nc, (AF.Sigmoid,))
    SET_GELU = _act_set_id(nc, (AF.Gelu_apprx_tanh,))

    x_ln = nc.dram_tensor("x_ln", [8, 128, HID], F32, kind="ExternalInput")
    x_res = nc.dram_tensor("x_res", [16, 64, HID], F32, kind="ExternalInput")
    wqkv_h = nc.dram_tensor("wqkv_h", [128, KC, 768], F32, kind="ExternalInput")
    wgate_h = nc.dram_tensor("wgate_h", [128, KC, 256], F32, kind="ExternalInput")
    wout_h = nc.dram_tensor("wout_h", [128, 2, HID], F32, kind="ExternalInput")
    rmsw_h = nc.dram_tensor("rmsw_h", [128, 2], F32, kind="ExternalInput")
    slope_h = nc.dram_tensor("slope_h", [1, 2], F32, kind="ExternalInput")
    wgp_h = nc.dram_tensor("wgp_h", [1024, NSLC], I32, kind="ExternalInput")
    wup_h = nc.dram_tensor("wup_h", [1024, NSLC], I32, kind="ExternalInput")
    wdp_h = nc.dram_tensor("wdp_h", [1024, HID], I32, kind="ExternalInput")
    sg_h = nc.dram_tensor("sg_h", [1024, NSLC], F32, kind="ExternalInput")
    su_h = nc.dram_tensor("su_h", [1024, NSLC], F32, kind="ExternalInput")
    sd_h = nc.dram_tensor("sd_h", [1024, HID], F32, kind="ExternalInput")
    ident_h = nc.dram_tensor("ident_h", [128, 128], BF16, kind="ExternalInput")
    idx1_h = nc.dram_tensor("idx1_h", [128, BLOCK], F32, kind="ExternalInput")
    dml_h = nc.dram_tensor("dml_h", [128, 2, BLOCK], F32, kind="ExternalInput")
    onesf_h = nc.dram_tensor("onesf_h", [1, 128], F32, kind="ExternalInput")
    onesc_h = nc.dram_tensor("onesc_h", [128, 1], BF16, kind="ExternalInput")

    out_sh = nc.dram_tensor("out_sh", [16, 64, HID], F32, kind="ExternalOutput")

    ag_in = [nc.dram_tensor(f"ag_in{h}", [KC, 128, TCH // 2], BF16) for h in range(2)]
    ag_out = [nc.dram_tensor(f"ag_out{h}", [NCHUNK, KC, 128, TCH // 2], BF16,
                             addr_space="Shared") for h in range(2)]
    ar_in = [nc.dram_tensor(f"ar_in{j}", [1, TCH], F32) for j in range(NCHUNK)]
    ar_out = [nc.dram_tensor(f"ar_out{j}", [1, TCH], F32) for j in range(NCHUNK)]
    wdec_d = nc.dram_tensor("wdec_d", [2, 16, KC, 128, 128], BF16)
    wddec_d = nc.dram_tensor("wddec_d", [KC, 128, HID], BF16)
    inter_d = nc.dram_tensor("inter_d", [16, 16, 128, 512], BF16)
    rs_in = [nc.dram_tensor(f"rs_in{j}", [512, HID], F32) for j in range(16)]
    rs_out = [nc.dram_tensor(f"rs_out{j}", [64, HID], F32) for j in range(16)]

    dbg = {}
    if stage == 1:
        dbg["xn_mine"] = nc.dram_tensor("dbg_xn_mine", [128, KC, TCH], BF16, kind="ExternalOutput")
        dbg["xn_c0"] = nc.dram_tensor("dbg_xn_c0", [128, KC, 512], BF16, kind="ExternalOutput")
    if stage == 2:
        dbg["qf"] = nc.dram_tensor("dbg_qf", [128, 2, TCH], BF16, kind="ExternalOutput")
        dbg["kf"] = nc.dram_tensor("dbg_kf", [128, 2, TCH], BF16, kind="ExternalOutput")
        dbg["vt"] = nc.dram_tensor("dbg_vt", [128, 8, 256], BF16, kind="ExternalOutput")
    if stage in (2, 3):
        dbg["o"] = nc.dram_tensor("dbg_o", [128, 2, TTOT], BF16, kind="ExternalOutput")
        dbg["ssq"] = nc.dram_tensor("dbg_ssq", [1, TTOT], F32, kind="ExternalOutput")
    if stage == 4:
        dbg["wdec"] = nc.dram_tensor("dbg_wdec", [2, 16, KC, 128, 128], BF16, kind="ExternalOutput")
        dbg["wddec"] = nc.dram_tensor("dbg_wddec", [KC, 128, HID], BF16, kind="ExternalOutput")
        dbg["inter"] = nc.dram_tensor("dbg_inter", [16, 16, 128, 512], BF16, kind="ExternalOutput")

    es = ExitStack()
    es123 = ExitStack()
    es4 = ExitStack()
    es5 = ExitStack()

    def S(stk, name, shape, dtype):
        return stk.enter_context(nc.sbuf_tensor(name, shape, dtype))

    with es:
        ident = S(es, "ident", [128, 128], BF16)
        idx1 = S(es, "idx1", [128, BLOCK], F32)
        dml = S(es, "dml", [128, 2, BLOCK], F32)
        onesf = S(es, "onesf", [1, 128], F32)
        onesc = S(es, "onesc", [128, 1], BF16)
        srow = S(es, "srow", [1, 2], F32)
        s_all = S(es, "s_all", [128, 2], F32)
        negs = S(es, "negs", [128, 2], F32)
        nb256 = S(es, "nb256", [128, 2], F32)
        qd = S(es, "qd", [128, 2, BLOCK], F32)
        kd = S(es, "kd", [128, 2, BLOCK], F32)
        dd = S(es, "dd", [128, 2, 2, BLOCK], F32)
        bd = S(es, "bd", [128, 2], F32)
        wout = S(es, "wout", [128, 2, HID], BF16)
        rmsw = S(es, "rmsw", [128, 2], F32)
        og_all = S(es, "og_all", [128, 2, TTOT], BF16)
        kv_f = S(es, "kv_f", [128, 2, 128], F32)
        kv_b = S(es, "kv_b", [128, 2, 128], BF16)
        ssq_ch = S(es, "ssq_ch", [1, TCH], F32)
        rvar = S(es, "rvar", [1, TCH], F32)
        rstd_j = S(es, "rstd_j", [1, TCH], F32)
        es_dec = ExitStack()
        DW = 512
        pk_t = [S(es_dec, f"pk_t{i}", [128, DW], I32) for i in range(2)]
        sct_t = [S(es_dec, f"sct_t{i}", [128, DW], F32) for i in range(2)]
        dec_u = S(es_dec, "dec_u", [128, DW], I32)
        dec_g = S(es_dec, "dec_g", [128, DW], I32)
        dec_ub = S(es_dec, "dec_ub", [128, DW], BF16)
        dec_a = S(es_dec, "dec_a", [128, DW], BF16)
        dec_b = S(es_dec, "dec_b", [128, DW], BF16)
        dec_p = S(es_dec, "dec_p", [128, DW], BF16)
        dec_w = S(es_dec, "dec_w", [128, DW], BF16)

        PB = [es.enter_context(nc.psum_tensor(f"PB{i}", [128, 1024], BF16)) for i in range(2)]
        PF = [es.enter_context(nc.psum_tensor(f"PF{i}", [128, 512], F32)) for i in range(6)]
        PF2, PF3, PF4, PF5, PF6, PF7 = PF

        DS = Ctr(es.enter_context(nc.semaphore("DS")))
        DG = Ctr(es.enter_context(nc.semaphore("DG")))
        TC = Ctr(es.enter_context(nc.semaphore("TC")))
        VC = Ctr(es.enter_context(nc.semaphore("VC")))
        SC = Ctr(es.enter_context(nc.semaphore("SC")))
        CC = Ctr(es.enter_context(nc.semaphore("CC")))

        bank_tok = {i: None for i in range(8)}

        stacks = []

        with nc.Block() as block:
            @block.sync
            def _(sync):
              try:
                T, V, Sc, G, Y = nc.tensor, nc.vector, nc.scalar, nc.gpsimd, nc.sync

                def w(eng, tok):
                    if tok is not None and tok[1] > 0:
                        eng.wait_ge(tok[0].sem, tok[1])

                def dma(q, ctr, out, in_):
                    ctr.inc(q.dma_start(out=out, in_=in_), 16)
                    return ctr.tok

                def preload(setid):
                    Sc.add_instruction(mybir.InstLoadActFuncSet(
                        name=nc.get_next_instruction_name(),
                        act_func_set_id=setid, ins=[], outs=[]))

                es_p1 = ExitStack()
                stacks.append(es_p1)
                wqkv = S(es123, "wqkv", [128, KC, 768], BF16)
                wgate = S(es123, "wgate", [128, KC, 256], BF16)
                xs = S(es_p1, "xs", [128, HID], F32)
                sqb = S(es_p1, "sqb", [128, HID], BF16)
                stat = S(es_p1, "stat", [128, 8], F32)
                xn_tm = S(es_p1, "xn_tm", [128, HID], BF16)
                xn_mine = S(es_p1, "xn_mine", [128, KC, TCH], BF16)

                preload(SET_NLE)
                t_c = None
                for dst, src in ((ident, ident_h), (idx1, idx1_h), (dml, dml_h),
                                 (onesf, onesf_h), (onesc, onesc_h), (srow, slope_h),
                                 (rmsw, rmsw_h)):
                    t_c = dma(Y, DS, dst[:], src.ap())
                t_wq = dma(G, DG, wqkv[:], wqkv_h.ap())
                t_wg = dma(G, DG, wgate[:], wgate_h.ap())
                t_wo = dma(G, DG, wout[:], wout_h.ap())

                w(T, t_c)
                TC.inc(T.matmul(PF2[:, 0:2], onesf[:], srow[:], start=True, stop=True))
                t = TC.tok
                w(V, t)
                VC.inc(V.tensor_copy(s_all[:], PF2[:, 0:2]))
                w(V, VC.tok)
                VC.inc(V.tensor_scalar(negs[:], s_all[:], -1.0, None, AL.mult))
                VC.inc(V.tensor_scalar(nb256[:], s_all[:], -256.0, None, AL.mult))
                VC.inc(V.memset(kv_f[:], 0.0))
                t_setup_v = VC.tok
                w(Sc, t_setup_v)
                for hh in range(2):
                    SC.inc(Sc.activation(qd[:, hh, :], idx1[:], AF.Exp, scale=negs[:, hh:hh + 1]))
                    SC.inc(Sc.activation(kd[:, hh, :], idx1[:], AF.Exp,
                                         scale=s_all[:, hh:hh + 1], bias=nb256[:, hh:hh + 1]))
                    SC.inc(Sc.activation(dd[:, hh, :, :], dml[:], AF.Exp, scale=negs[:, hh:hh + 1]))
                    SC.inc(Sc.activation(bd[:, hh:hh + 1], s_all[:, hh:hh + 1], AF.Exp, scale=-256.0))
                w(Sc, t_setup_v)
                SC.inc(Sc.copy(kv_b[:], kv_f[:]))
                t_decay = SC.tok
                bank_tok[2] = t_setup_v

                t_xs_v = None
                t_xs_s = None
                t_ag2 = [None, None]
                for u in range(8):
                    w(Y, t_xs_v)
                    w(Y, t_xs_s)
                    t_x = dma(Y, DS, xs[:], x_ln.ap()[u])
                    w(V, t_x)
                    VC.inc(V.reduce_sum(stat[:, 0:1], xs[:], axis=AX))
                    t_sum = VC.tok
                    w(Sc, t_x)
                    SC.inc(Sc.activation(sqb[:], xs[:], AF.Square, accum_out=stat[:, 1:2]))
                    t_ssq = SC.tok
                    t_xs_s = t_ssq
                    w(V, t_sum)
                    VC.inc(V.tensor_scalar(stat[:, 2:3], stat[:, 0:1], 1.0 / HID, None, AL.mult))
                    w(V, VC.tok)
                    VC.inc(V.tensor_tensor(stat[:, 3:4], stat[:, 2:3], stat[:, 2:3], AL.mult))
                    w(V, t_ssq)
                    VC.inc(V.tensor_scalar(stat[:, 4:5], stat[:, 1:2], 1.0 / HID, LN_EPS, AL.mult, AL.add))
                    w(V, VC.tok)
                    VC.inc(V.tensor_tensor(stat[:, 4:5], stat[:, 4:5], stat[:, 3:4], AL.subtract))
                    t_var = VC.tok
                    w(Sc, t_var)
                    SC.inc(Sc.activation(stat[:, 5:6], stat[:, 4:5], AF.Ln))
                    w(Sc, SC.tok)
                    SC.inc(Sc.activation(stat[:, 6:7], stat[:, 5:6], AF.Exp, scale=-0.5))
                    t_rstd = SC.tok
                    w(V, t_rstd)
                    VC.inc(V.tensor_scalar(xn_tm[:], xs[:], stat[:, 2:3], stat[:, 6:7],
                                           AL.subtract, AL.mult))
                    t_norm = VC.tok
                    t_xs_v = t_norm
                    w(T, t_norm)
                    for q in range(KC):
                        b = q % 2
                        w(T, bank_tok[b])
                        base = 256 * (q // 2) + (q % 2)
                        TC.inc(T.transpose(PB[b][:, 0:128], xn_tm[:, base:base + 255 + 1 - (q % 2):2], ident[:]))
                        w(Sc, TC.tok)
                        SC.inc(Sc.copy(xn_mine[:, q, 128 * u:128 * u + 128], PB[b][:, 0:128]))
                        bank_tok[b] = SC.tok
                    if u == 3 or u == 7:
                        hf_ = u // 4
                        w(G, SC.tok)
                        t_agst = dma(G, DG, ag_in[hf_].ap().rearrange("q p t -> p q t"),
                                     xn_mine[:, :, 512 * hf_:512 * hf_ + 512])
                        w(G, t_agst)
                        CC.inc(G.collective_compute(
                            "AllGather", AL.bypass, replica_groups=[list(range(8))],
                            ins=[ag_in[hf_].ap().opt()], outs=[ag_out[hf_].ap().opt()]))
                        t_ag2[hf_] = CC.tok
                t_ag = t_ag2[1]

                if stage == 1:
                    dma(G, DG, dbg["xn_mine"].ap(), xn_mine[:])
                es_p1.close()
                xn_ch = [S(es123, f"xn_ch{i}", [128, KC, 512], BF16) for i in range(2)]
                q_fm2 = [S(es123, f"q_fm{i}", [128, 2, TCH], BF16) for i in range(2)]
                k_fm2 = [S(es123, f"k_fm{i}", [128, 2, TCH], BF16) for i in range(2)]
                v_tm2 = [S(es123, f"v_tm{i}", [128, 8, 256], BF16) for i in range(2)]
                g_ch = [S(es123, f"g_ch{i}", [128, 2, TCH], BF16) for i in range(2)]
                qt_s = S(es123, "qt_s", [128, BLOCK], BF16)
                kt_s = S(es123, "kt_s", [128, BLOCK], BF16)
                ktT = S(es123, "ktT", [128, 2, 128], BF16)
                sc_sb = S(es123, "sc_sb", [128, 2, BLOCK], BF16)
                o2_s = S(es123, "o2_s", [128, BLOCK], BF16)
                if stage == 1:
                    w(Y, t_ag)
                    t_l = dma(Y, DS, xn_ch[0][:],
                              ag_out[0].ap()[0].rearrange("q p t -> p q t"))
                    w(G, t_l)
                    dma(G, DG, dbg["xn_c0"].ap(), xn_ch[0][:])
                    w(G, DG.tok); w(Y, DS.tok); w(Sc, SC.tok); w(V, VC.tok); w(T, TC.tok)
                    return

                jobs = [(m, kk, qt) for m in range(2) for kk in range(8) for qt in range(4)]
                jobs += [(2, kk, qt) for kk in range(8) for qt in range(4)]
                N_GU_JOBS = 64
                ji = [0]
                pslot = [0]
                slot_tok = [None, None]
                dec_w_store = [None]

                def emit_decode_one():
                    if ji[0] >= len(jobs):
                        return
                    mat, kk, qt = jobs[ji[0]]
                    ji[0] += 1
                    sl = pslot[0]
                    pslot[0] ^= 1
                    pk, sct = pk_t[sl], sct_t[sl]
                    src_p = (wgp_h, wup_h, wdp_h)[mat]
                    src_s = (sg_h, su_h, sd_h)[mat]
                    W = DW
                    w(Y, slot_tok[sl])
                    dma(Y, DS, pk[:, :], src_p.ap()[128 * kk:128 * kk + 128, W * qt:W * qt + W])
                    t_pk = dma(Y, DS, sct[:, :], src_s.ap()[128 * kk:128 * kk + 128, W * qt:W * qt + W])
                    w(V, t_pk)
                    for delta in range(2):
                        if delta == 0:
                            VC.inc(V.tensor_single_scalar(dec_u[:, :], pk[:, :], 7, AL.bitwise_and))
                            VC.inc(V.tensor_scalar(dec_g[:, :], pk[:, :], 3, 1,
                                                   AL.logical_shift_right, AL.bitwise_and))
                        else:
                            w(V, VC.tok)
                            VC.inc(V.tensor_scalar(dec_u[:, :], pk[:, :], 4, 7,
                                                   AL.logical_shift_right, AL.bitwise_and))
                            VC.inc(V.tensor_scalar(dec_g[:, :], pk[:, :], 7, 1,
                                                   AL.logical_shift_right, AL.bitwise_and))
                        w(V, VC.tok)
                        VC.inc(V.tensor_copy(dec_ub[:, :], dec_u[:, :]))
                        VC.inc(V.tensor_scalar(dec_p[:, :], dec_g[:, :], -2.0, 1.0, AL.mult, AL.add))
                        w(V, VC.tok)
                        VC.inc(V.tensor_scalar(dec_a[:, :], dec_ub[:, :], 2.0, -8.0, AL.mult, AL.add))
                        w(V, VC.tok)
                        VC.inc(V.scalar_tensor_tensor(dec_b[:, :], dec_ub[:, :], -2.0, dec_a[:, :],
                                                      AL.add, AL.max))
                        w(V, VC.tok)
                        VC.inc(V.scalar_tensor_tensor(dec_a[:, :], dec_ub[:, :], 0.5, dec_b[:, :],
                                                      AL.mult, AL.max))
                        w(V, VC.tok)
                        VC.inc(V.tensor_tensor(dec_b[:, :], dec_a[:, :], dec_p[:, :], AL.mult))
                        w(V, VC.tok)
                        w(V, dec_w_store[0])
                        VC.inc(V.tensor_tensor(dec_w[:, :], dec_b[:, :], sct[:, :], AL.mult))
                        t_w = VC.tok
                        w(G, t_w)
                        q = 2 * kk + delta
                        if mat < 2:
                            dec_w_store[0] = dma(G, DG,
                                wdec_d.ap()[mat, 4 * qt:4 * qt + 4, q].rearrange("nt p n -> p nt n"),
                                dec_w[:, :])
                        else:
                            dec_w_store[0] = dma(G, DG, wddec_d.ap()[q, :, W * qt:W * qt + W], dec_w[:, :])
                    slot_tok[sl] = VC.tok

                xn_cons = [None, None]
                g_cons = [None, None]
                attn_cons = [None, None]
                qkv_tok = [None, None]
                ar_tok = [None] * NCHUNK
                ssq_dma_tok = [None]
                gate_guard = {"ln": None, "bc": None}
                sA = {"kvb": t_decay, "qt": None, "kt": None, "ktT": None,
                      "scsb": None, "sq": None, "b3": []}

                def emit_gating(jm):
                    gsl = g_ch[jm % 2]
                    w(Y, ar_tok[jm])
                    w(Y, gate_guard["ln"])
                    t_aro = dma(Y, DS, rvar[:], ar_out[jm].ap())
                    preload(SET_NLE)
                    w(V, t_aro)
                    VC.inc(V.tensor_scalar(rvar[:], rvar[:], 1.0 / HID, RMS_EPS, AL.mult, AL.add))
                    w(Sc, VC.tok)
                    w(Sc, gate_guard["bc"])
                    SC.inc(Sc.activation(rstd_j[:], rvar[:], AF.Ln))
                    gate_guard["ln"] = SC.tok
                    w(Sc, SC.tok)
                    SC.inc(Sc.activation(rstd_j[:], rstd_j[:], AF.Exp, scale=-0.5))
                    t_rst = SC.tok
                    for th in range(2):
                        bk = 4 + th
                        w(T, t_rst)
                        w(T, bank_tok[bk])
                        TC.inc(T.matmul(PF[bk - 2][:, :], onesf[:], rstd_j[0:1, 512 * th:512 * th + 512],
                                        start=True, stop=True))
                        t_bc = TC.tok
                        gate_guard["bc"] = t_bc
                        w(V, t_bc)
                        for hh in range(2):
                            sli = slice(TCH * jm + 512 * th, TCH * jm + 512 * th + 512)
                            VC.inc(V.tensor_tensor(og_all[:, hh, sli], og_all[:, hh, sli],
                                                   gsl[:, hh, 512 * th:512 * th + 512], AL.mult))
                            w(V, VC.tok)
                            VC.inc(V.scalar_tensor_tensor(og_all[:, hh, sli], og_all[:, hh, sli],
                                                          rmsw[:, hh:hh + 1], PF[bk - 2][:, :],
                                                          AL.mult, AL.mult))
                        bank_tok[bk] = VC.tok
                    g_cons[jm % 2] = VC.tok

                rot_banks = [4, 5, 6]

                def gen_proj(j):
                    sl = j % 2
                    qf, kf, vt, gsl = q_fm2[sl], k_fm2[sl], v_tm2[sl], g_ch[sl]
                    t_xl = [None, None]
                    for th in range(2):
                        w(Y, t_ag2[th])
                        w(Y, xn_cons[th])
                        t_xl[th] = dma(Y, DS, xn_ch[th][:],
                                       ag_out[th].ap()[j].rearrange("q p t -> p q t"))
                    if j == 0:
                        T.wait_ge(DG.sem, t_wo[1])
                    preload(SET_SILU)
                    ri = [0]

                    def nxt_bank():
                        bk = rot_banks[ri[0] % 3]
                        ri[0] += 1
                        return bk, PF[bk - 2]

                    for cc in range(4):
                        for th in range(2):
                            bk, pf = nxt_bank()
                            w(T, t_xl[th])
                            w(T, bank_tok[bk])
                            for q in range(KC):
                                mmi = T.matmul(pf[:, :], wqkv[:, q, 128 * cc:128 * cc + 128],
                                               xn_ch[th][:, q, :], start=(q == 0), stop=(q == KC - 1))
                                if q == KC - 1:
                                    TC.inc(mmi)
                            t_mm = TC.tok
                            xn_cons[th] = t_mm
                            hh, isq = cc % 2, cc < 2
                            w(Sc, t_mm)
                            w(Sc, attn_cons[sl])
                            SC.inc(Sc.activation((qf if isq else kf)[:, hh, 512 * th:512 * th + 512],
                                                 pf[:, :], AF.Silu))
                            bank_tok[bk] = SC.tok
                            yield
                    for u in range(8):
                        th = u // 4
                        bk, pf = nxt_bank()
                        w(T, t_xl[th])
                        w(T, bank_tok[bk])
                        for q in range(KC):
                            mmi = T.matmul(pf[:, 0:256], xn_ch[th][:, q, 128 * (u % 4):128 * (u % 4) + 128],
                                           wqkv[:, q, 512:768], start=(q == 0), stop=(q == KC - 1))
                            if q == KC - 1:
                                TC.inc(mmi)
                        t_mm = TC.tok
                        xn_cons[th] = t_mm
                        w(Sc, t_mm)
                        w(Sc, attn_cons[sl])
                        SC.inc(Sc.activation(vt[:, u, :], pf[:, 0:256], AF.Silu))
                        bank_tok[bk] = SC.tok
                        yield
                    qkv_tok[sl] = SC.tok
                    preload(SET_SIG)
                    for hh in range(2):
                        for th in range(2):
                            bk, pf = nxt_bank()
                            w(T, t_xl[th])
                            w(T, bank_tok[bk])
                            for q in range(KC):
                                mmi = T.matmul(pf[:, :], wgate[:, q, 128 * hh:128 * hh + 128],
                                               xn_ch[th][:, q, :], start=(q == 0), stop=(q == KC - 1))
                                if q == KC - 1:
                                    TC.inc(mmi)
                            t_mm = TC.tok
                            xn_cons[th] = t_mm
                            w(Sc, t_mm)
                            w(Sc, g_cons[sl])
                            SC.inc(Sc.activation(gsl[:, hh, 512 * th:512 * th + 512], pf[:, :], AF.Sigmoid))
                            bank_tok[bk] = SC.tok
                            yield

                def gen_attn(jm):
                    sl = jm % 2
                    qf, kf, vt = q_fm2[sl], k_fm2[sl], v_tm2[sl]
                    if jm == 4:
                        w(V, attn_cons[1])
                        VC.inc(V.memset(kv_f[:], 0.0))
                        w(Sc, VC.tok)
                        SC.inc(Sc.copy(kv_b[:], kv_f[:]))
                        sA["kvb"] = SC.tok
                    for blk in range(4):
                        l0 = BLOCK * blk
                        for hh in range(2):
                            gl = TCH * jm + l0
                            w(V, qkv_tok[sl])
                            w(V, sA["qt"])
                            w(V, sA["kt"])
                            VC.inc(V.tensor_tensor(qt_s[:], qf[:, hh, l0:l0 + BLOCK], qd[:, hh, :], AL.mult))
                            VC.inc(V.tensor_tensor(kt_s[:], kf[:, hh, l0:l0 + BLOCK], kd[:, hh, :], AL.mult))
                            t_qtkt = VC.tok
                            w(T, qkv_tok[sl])
                            w(T, bank_tok[2])
                            for mh in range(2):
                                TC.inc(T.matmul(PF2[:, 256 * mh:256 * mh + 256],
                                                kf[:, hh, l0 + 128 * mh:l0 + 128 * mh + 128],
                                                qf[:, hh, l0:l0 + BLOCK], start=True, stop=True))
                            t_sc_mm = TC.tok
                            yield
                            w(V, t_sc_mm)
                            w(V, sA["scsb"])
                            for mh in range(2):
                                VC.inc(V.tensor_tensor(sc_sb[:, mh, :], PF2[:, 256 * mh:256 * mh + 256],
                                                       dd[:, hh, mh, :], AL.mult))
                            t_scm = VC.tok
                            bank_tok[2] = t_scm
                            w(T, sA["kvb"])
                            w(T, t_qtkt)
                            for tk in sA["b3"]:
                                w(T, tk)
                            TC.inc(T.matmul(PF3[:, 0:256], kv_b[:, hh, :], qt_s[:], start=True, stop=False))
                            sA["qt"] = TC.tok
                            w(T, t_scm)
                            for mh in range(2):
                                TC.inc(T.matmul(PF3[:, 0:256], vt[:, 2 * blk + mh, 128 * hh:128 * hh + 128],
                                                sc_sb[:, mh, :], start=False, stop=(mh == 1)))
                            t_o_mm = TC.tok
                            sA["scsb"] = t_o_mm
                            w(T, bank_tok[1])
                            TC.inc(T.transpose(PB[1][:, 0:128], kt_s[:, 0:128], ident[:]))
                            TC.inc(T.transpose(PB[1][:, 128:256], kt_s[:, 128:256], ident[:]))
                            t_tr = TC.tok
                            sA["kt"] = t_tr
                            w(Sc, t_tr)
                            w(Sc, sA["ktT"])
                            SC.inc(Sc.copy(ktT[:, 0, :], PB[1][:, 0:128]))
                            SC.inc(Sc.copy(ktT[:, 1, :], PB[1][:, 128:256]))
                            t_ktT = SC.tok
                            bank_tok[1] = t_ktT
                            w(T, t_ktT)
                            for mh in range(2):
                                TC.inc(T.matmul(PF3[:, 256:384], ktT[:, mh, :],
                                                vt[:, 2 * blk + mh, 128 * hh:128 * hh + 128],
                                                start=(mh == 0), stop=(mh == 1)))
                            t_a_mm = TC.tok
                            sA["ktT"] = t_a_mm
                            w(Sc, t_a_mm)
                            SC.inc(Sc.copy(og_all[:, hh, gl:gl + BLOCK], PF3[:, 0:256]))
                            t_oev = SC.tok
                            w(V, t_a_mm)
                            w(V, t_oev)
                            w(V, sA["kvb"])
                            VC.inc(V.scalar_tensor_tensor(kv_f[:, hh, :], kv_f[:, hh, :], bd[:, hh:hh + 1],
                                                          PF3[:, 256:384], AL.mult, AL.add))
                            t_kvf = VC.tok
                            sA["b3"] = [t_oev, t_kvf]
                            w(Sc, t_kvf)
                            SC.inc(Sc.copy(kv_b[:, hh, :], kv_f[:, hh, :]))
                            sA["kvb"] = SC.tok
                            w(V, t_oev)
                            w(V, sA["sq"])
                            VC.inc(V.tensor_tensor(o2_s[:], og_all[:, hh, gl:gl + BLOCK],
                                                   og_all[:, hh, gl:gl + BLOCK], AL.mult))
                            t_o2 = VC.tok
                            w(T, t_o2)
                            if blk % 2 == 0 and hh == 0:
                                w(T, bank_tok[7])
                            TC.inc(T.matmul(PF7[0:1, 256 * (blk % 2):256 * (blk % 2) + 256],
                                            onesc[:], o2_s[:], start=(hh == 0), stop=(hh == 1)))
                            sA["sq"] = TC.tok
                            attn_cons[sl] = TC.tok
                            yield
                        if blk % 2 == 1:
                            w(V, sA["sq"])
                            w(V, ssq_dma_tok[0])
                            VC.inc(V.tensor_copy(ssq_ch[0:1, 512 * (blk // 2):512 * (blk // 2) + 512],
                                                 PF7[0:1, :]))
                            bank_tok[7] = VC.tok
                    t_sev = VC.tok
                    w(G, t_sev)
                    t_ast = dma(G, DG, ar_in[jm].ap(), ssq_ch[0:1, :])
                    ssq_dma_tok[0] = t_ast
                    w(G, t_ast)
                    CC.inc(G.collective_compute(
                        "AllReduce", AL.add, replica_groups=[list(range(8))],
                        ins=[ar_in[jm].ap().opt()], outs=[ar_out[jm].ap().opt()]))
                    ar_tok[jm] = CC.tok

                ga = None
                for j in range(NCHUNK):
                    gp = gen_proj(j)
                    if ga is None:
                        for _ in gp:
                            pass
                    elif INTERLEAVE:
                        kq = 0
                        for _ in ga:
                            next(gp, None)
                            next(gp, None)
                            if kq % 2 == 1:
                                emit_decode_one()
                            kq += 1
                        for _ in gp:
                            pass
                        if stage != 2:
                            emit_gating(j - 1)
                    else:
                        kq = 0
                        for _ in ga:
                            if kq % 2 == 1:
                                emit_decode_one()
                            kq += 1
                        for _ in gp:
                            pass
                        if stage != 2:
                            emit_gating(j - 1)
                    if stage == 2 and j == 0:
                        w(G, qkv_tok[0])
                        dma(G, DG, dbg["qf"].ap(), q_fm2[0][:])
                        dma(G, DG, dbg["kf"].ap(), k_fm2[0][:])
                        dma(G, DG, dbg["vt"].ap(), v_tm2[0][:])
                    ga = gen_attn(j)
                kq = 0
                for _ in ga:
                    if kq % 2 == 1:
                        emit_decode_one()
                    kq += 1
                if stage != 2:
                    emit_gating(NCHUNK - 1)

                if stage in (2, 3):
                    w(G, VC.tok)
                    w(G, SC.tok)
                    dma(G, DG, dbg["o"].ap(), og_all[:])
                    for jq in range(NCHUNK):
                        dma(G, DG, dbg["ssq"].ap()[0:1, TCH * jq:TCH * jq + TCH], ar_in[jq].ap())
                    w(G, DG.tok); w(Y, DS.tok); w(Sc, SC.tok); w(V, VC.tok); w(T, TC.tok)
                    return

                while ji[0] < N_GU_JOBS:
                    emit_decode_one()
                t_gu_dec = DG.tok

                es123.close()
                xn_tc = [S(es4, f"xn_tc{i}", [128, KC, 512], BF16) for i in range(2)]
                wl_g = [S(es4, f"wl_g{i}", [128, KC, 4, 128], BF16) for i in range(2)]
                wl_u = [S(es4, f"wl_u{i}", [128, KC, 4, 128], BF16) for i in range(2)]
                gel_s = S(es4, "gel_s", [128, 4, 512], BF16)
                up_s = S(es4, "up_s", [128, 512], BF16)
                int_s = [S(es4, f"int_s{i}", [128, 4, 512], BF16) for i in range(2)]

                t_bar = (TC.tok, VC.tok, SC.tok)
                for tok in t_bar:
                    w(Y, tok)

                preload(SET_GELU)
                xtc_cons = [None, None]
                wl_cons = [None, None]
                ist_tok = [None, None]
                gel_cons = [None] * 4
                for ntg in range(4):
                    ws = ntg % 2
                    w(Y, wl_cons[ws])
                    if ntg == 0:
                        w(Y, t_gu_dec)
                    for nt4 in range(4):
                        t_wgl = dma(Y, DS, wl_g[ws][:, :, nt4, :],
                                    wdec_d.ap()[0, 4 * ntg + nt4].rearrange("q p n -> p q n"))
                        t_wul = dma(Y, DS, wl_u[ws][:, :, nt4, :],
                                    wdec_d.ap()[1, 4 * ntg + nt4].rearrange("q p n -> p q n"))
                    for tc in range(16):
                        ts_ = tc % 2
                        jj, th = tc // 2, tc % 2
                        w(Y, xtc_cons[ts_])
                        t_x = dma(Y, DS, xn_tc[ts_][:],
                                  ag_out[th].ap()[jj].rearrange("q p t -> p q t"))
                        isl = tc % 2
                        w(T, t_wul)
                        w(T, t_x)
                        for nt4 in range(4):
                            bk = 4 + (nt4 % 2) * 2
                            pf = PF[2 + (nt4 % 2) * 2]
                            w(T, bank_tok[bk])
                            for q in range(KC):
                                mmi = T.matmul(pf[:, :], wl_g[ws][:, q, nt4, :],
                                               xn_tc[ts_][:, q, :], start=(q == 0), stop=(q == KC - 1))
                                if q == KC - 1:
                                    TC.inc(mmi)
                            t_mm = TC.tok
                            w(Sc, t_mm)
                            w(Sc, gel_cons[nt4])
                            SC.inc(Sc.activation(gel_s[:, nt4, :], pf[:, :], AF.Gelu_apprx_tanh))
                            bank_tok[bk] = SC.tok
                            bku = 5 + (nt4 % 2) * 2
                            pfu = PF[3 + (nt4 % 2) * 2]
                            w(T, bank_tok[bku])
                            for q in range(KC):
                                mmi = T.matmul(pfu[:, :], wl_u[ws][:, q, nt4, :],
                                               xn_tc[ts_][:, q, :], start=(q == 0), stop=(q == KC - 1))
                                if q == KC - 1:
                                    TC.inc(mmi)
                            t_mmu = TC.tok
                            xtc_cons[ts_] = t_mmu
                            wl_cons[ws] = t_mmu
                            w(V, t_mmu)
                            w(V, ist_tok[isl])
                            VC.inc(V.tensor_copy(up_s[:, :], pfu[:, :]))
                            w(V, VC.tok)
                            w(V, bank_tok[bk])
                            VC.inc(V.tensor_tensor(int_s[isl][:, nt4, :], gel_s[:, nt4, :], up_s[:, :], AL.mult))
                            bank_tok[bku] = VC.tok
                            gel_cons[nt4] = VC.tok
                        t_int = VC.tok
                        w(G, t_int)
                        ist_tok[isl] = dma(G, DG,
                                           inter_d.ap()[4 * ntg:4 * ntg + 4, tc].rearrange("nt p t -> p nt t"),
                                           int_s[isl][:, :, :])
                        if tc % 2 == 0:
                            emit_decode_one()
                while ji[0] < len(jobs):
                    emit_decode_one()

                if stage == 4:
                    w(G, DG.tok)
                    dma(G, DG, dbg["wdec"].ap(), wdec_d.ap())
                    dma(G, DG, dbg["wddec"].ap(), wddec_d.ap())
                    dma(G, DG, dbg["inter"].ap(), inter_d.ap())
                    w(G, DG.tok); w(Y, DS.tok); w(Sc, SC.tok); w(V, VC.tok); w(T, TC.tok)
                    return

                es4.close()
                es_dec.close()
                wd_res = S(es5, "wd_res", [128, KC, HID], BF16)
                int_l = [S(es5, f"int_l{i}", [128, KC, 512], BF16) for i in range(2)]
                rs_st = [S(es5, f"rs_st{i}", [128, HID], F32) for i in range(2)]
                xr_s = S(es5, "xr_s", [64, HID], F32)
                ro_s = S(es5, "ro_s", [64, HID], F32)
                ou_s = S(es5, "ou_s", [64, HID], F32)

                t_bar2 = (TC.tok, VC.tok, SC.tok)
                for tok in t_bar2:
                    w(Y, tok)
                w(Y, DG.tok)
                t_wd = dma(Y, DS, wd_res[:], wddec_d.ap().rearrange("q p h -> p q h"))

                il_cons = [None, None]
                rs_dma = [None, None]
                rs_tok = [None] * 16
                p6_state = {"add": None, "ost": None}

                def p6(jj):
                    w(Y, rs_tok[jj])
                    w(Y, p6_state["add"])
                    t_ro = dma(Y, DS, ro_s[:], rs_out[jj].ap())
                    t_xr = dma(Y, DS, xr_s[:], x_res.ap()[jj])
                    w(V, t_xr)
                    w(V, p6_state["ost"])
                    VC.inc(V.tensor_tensor(ou_s[:], ro_s[:], xr_s[:], AL.add))
                    p6_state["add"] = VC.tok
                    w(G, p6_state["add"])
                    p6_state["ost"] = dma(G, DG, out_sh.ap()[jj], ou_s[:])

                for tc in range(16):
                    ts_ = tc % 2
                    w(Y, il_cons[ts_])
                    t_il = dma(Y, DS, int_l[ts_][:], inter_d.ap()[:, tc].rearrange("nt p t -> p nt t"))
                    for s in range(4):
                        tt = 4 * tc + s
                        rsl = tt % 2
                        w(T, t_il)
                        w(T, t_wd)
                        pfm = [PF[(4 * tt + h4) % 6] for h4 in range(4)]
                        for h4 in range(4):
                            w(T, bank_tok[2 + (4 * tt + h4) % 6])
                        for q in range(KC):
                            for h4 in range(4):
                                T.matmul(pfm[h4][:, :], int_l[ts_][:, q, 128 * s:128 * s + 128],
                                         wd_res[:, q, 512 * h4:512 * h4 + 512],
                                         start=(q == 0), stop=False)
                        for jc in range(2):
                            for h4 in range(4):
                                mmi = T.matmul(pfm[h4][:, :], og_all[:, jc, 128 * tt:128 * tt + 128],
                                               wout[:, jc, 512 * h4:512 * h4 + 512],
                                               start=False, stop=(jc == 1))
                                if jc == 1:
                                    TC.inc(mmi)
                        t_mm = TC.tok
                        il_cons[ts_] = t_mm
                        w(Sc, t_mm)
                        w(Sc, rs_dma[rsl])
                        SC.inc(Sc.copy(rs_st[rsl][:, 0:512], pfm[0][:, :]))
                        SC.inc(Sc.copy(rs_st[rsl][:, 512:1024], pfm[1][:, :]))
                        bank_tok[2 + (4 * tt) % 6] = SC.tok
                        bank_tok[2 + (4 * tt + 1) % 6] = SC.tok
                        t_ev_s = SC.tok
                        w(V, t_mm)
                        w(V, rs_dma[rsl])
                        VC.inc(V.tensor_copy(rs_st[rsl][:, 1024:1536], pfm[2][:, :]))
                        VC.inc(V.tensor_copy(rs_st[rsl][:, 1536:2048], pfm[3][:, :]))
                        bank_tok[2 + (4 * tt + 2) % 6] = VC.tok
                        bank_tok[2 + (4 * tt + 3) % 6] = VC.tok
                        t_ev_v = VC.tok
                        w(G, t_ev_s)
                        w(G, t_ev_v)
                        jj, row = tt // 4, tt % 4
                        rs_dma[rsl] = dma(G, DG, rs_in[jj].ap()[128 * row:128 * row + 128, :],
                                          rs_st[rsl][:, :])
                    w(G, rs_dma[0])
                    w(G, rs_dma[1])
                    CC.inc(G.collective_compute(
                        "ReduceScatter", AL.add, replica_groups=[list(range(8))],
                        ins=[rs_in[tc].ap().opt()], outs=[rs_out[tc].ap().opt()]))
                    rs_tok[tc] = CC.tok
                    if tc >= 1:
                        p6(tc - 1)
                p6(15)
                w(G, DG.tok)
                w(Y, DS.tok)
                w(Sc, SC.tok)
                w(V, VC.tok)
                w(T, TC.tok)
              finally:
                for st in stacks + [es5, es4, es123, es_dec]:
                    try:
                        st.close()
                    except Exception:
                        pass
    return nc, dbg


def _host_prep(inputs):
    x = np.ascontiguousarray(np.asarray(inputs["L0_attn_hidden_states"], dtype=np.float32).reshape(TTOT, HID))
    slope = np.asarray(inputs["L0_attn_slope_rate"], dtype=np.float32).reshape(H)
    w_qkv = np.asarray(inputs["L0_attn_w_qkv"], dtype=np.float32)
    w_gate = np.asarray(inputs["L0_attn_w_gate"], dtype=np.float32)
    w_out = np.asarray(inputs["L0_attn_w_out"], dtype=np.float32)
    rms_w = np.asarray(inputs["L0_attn_rms_weight"], dtype=np.float32)
    wgp = np.asarray(inputs["L0_ffn_gate_W_packed"], dtype=np.int32)
    sgs = np.asarray(inputs["L0_ffn_gate_scales"], dtype=np.float32)
    wup = np.asarray(inputs["L0_ffn_up_W_packed"], dtype=np.int32)
    sus = np.asarray(inputs["L0_ffn_up_scales"], dtype=np.float32)
    wdp = np.asarray(inputs["L0_ffn_down_W_packed"], dtype=np.int32)
    sds = np.asarray(inputs["L0_ffn_down_scales"], dtype=np.float32)

    wqkv_p = w_qkv[PERM]
    wgate_p = w_gate[PERM]
    wgpT = np.ascontiguousarray(wgp.T)
    wupT = np.ascontiguousarray(wup.T)
    wdpT = np.ascontiguousarray(wdp.T)
    sgsT = np.ascontiguousarray(sgs.T)
    susT = np.ascontiguousarray(sus.T)
    sdsT = np.ascontiguousarray(sds.T)

    ident = np.eye(128, dtype=ml_dtypes.bfloat16)
    idx1 = np.ascontiguousarray(np.tile(np.arange(1, BLOCK + 1, dtype=np.float32), (128, 1)))
    dmlt = np.empty((128, 2, BLOCK), dtype=np.float32)
    for qq in range(2):
        m = 128 * qq + np.arange(128)[:, None]
        l = np.arange(BLOCK)[None, :]
        dmlt[:, qq, :] = np.where(l >= m, (l - m).astype(np.float32), BIGM)
    onesf = np.ones((1, 128), dtype=np.float32)
    onesc = np.ones((128, 1), dtype=ml_dtypes.bfloat16)

    maps = []
    xr4 = x.reshape(16, 8, 64, HID)
    for c in range(8):
        cols = np.r_[256 * c:256 * c + 256, 2048 + 256 * c:2048 + 256 * c + 256,
                     4096 + 256 * c:4096 + 256 * c + 256]
        nsl = 2048 * c + PERM
        m = dict(
            x_ln=np.ascontiguousarray(x[1024 * c:1024 * c + 1024].reshape(8, 128, HID)),
            x_res=np.ascontiguousarray(xr4[:, c]),
            wqkv_h=np.ascontiguousarray(wqkv_p[:, cols].reshape(KC, 128, 768).transpose(1, 0, 2)),
            wgate_h=np.ascontiguousarray(
                wgate_p[:, 256 * c:256 * c + 256].reshape(KC, 128, 256).transpose(1, 0, 2)),
            wout_h=np.ascontiguousarray(w_out[256 * c:256 * c + 256].reshape(2, 128, HID).transpose(1, 0, 2)),
            rmsw_h=np.ascontiguousarray(rms_w[256 * c:256 * c + 256].reshape(2, 128).T),
            slope_h=np.ascontiguousarray(slope[[2 * c, 2 * c + 1]].reshape(1, 2)),
            wgp_h=np.ascontiguousarray(wgpT[:, nsl]),
            wup_h=np.ascontiguousarray(wupT[:, nsl]),
            wdp_h=np.ascontiguousarray(wdpT[1024 * c:1024 * c + 1024]),
            sg_h=np.ascontiguousarray(np.repeat(sgsT[:, nsl], 16, axis=0)),
            su_h=np.ascontiguousarray(np.repeat(susT[:, nsl], 16, axis=0)),
            sd_h=np.ascontiguousarray(np.repeat(sdsT[64 * c:64 * c + 64], 16, axis=0)),
            ident_h=ident, idx1_h=idx1, dml_h=dmlt, onesf_h=onesf, onesc_h=onesc,
        )
        maps.append(m)
    return maps


def kernel(**inputs):
    maps = _host_prep(inputs)
    nc, dbg = build(STAGE)
    res = run_bass_kernel_spmd(nc, maps, core_ids=list(range(8)))
    if STAGE < 5:
        return res
    out = np.empty((TTOT, HID), dtype=np.float32)
    for c in range(8):
        sh = res.results[c]["out_sh"]
        for h in range(16):
            out[512 * h + 64 * c:512 * h + 64 * c + 64] = sh[h]
    return out.reshape(B, N, HID)


# version 13
# speedup vs baseline: 1.0677x; 1.0152x over previous
import numpy as np
import ml_dtypes
from contextlib import ExitStack
import concourse.bass as bass
import concourse.mybir as mybir
from concourse.bass_utils import run_bass_kernel_spmd

F32 = mybir.dt.float32
BF16 = mybir.dt.bfloat16
I32 = mybir.dt.int32
AL = mybir.AluOpType
AF = mybir.ActivationFunctionType
AX = mybir.AxisListType.X

B, N, HID, H, D = 2, 4096, 2048, 16, 128
TTOT = B * N
NCHUNK = 8
TCH = TTOT // NCHUNK
BLOCK = 256
KC = HID // 128
NSLC = 2048
LN_EPS = 1e-5
RMS_EPS = 1e-6
BIGM = 1.0e9

STAGE = 5
INTERLEAVE_FLAG = [True]


def _perm(n):
    nc2 = n // 256
    idx = np.empty(n, dtype=np.int64)
    for j2 in range(2 * nc2):
        base = 256 * (j2 // 2) + (j2 % 2)
        idx[j2 * 128:(j2 + 1) * 128] = base + 2 * np.arange(128)
    return idx


PERM = _perm(HID)


class Ctr:
    __slots__ = ("sem", "n")

    def __init__(self, sem):
        self.sem = sem
        self.n = 0

    def inc(self, instr, k=1):
        instr.then_inc(self.sem, k)
        self.n += k
        return self.n

    @property
    def tok(self):
        return (self, self.n)


def _act_set_id(nc, want):
    from concourse.hw_specs import get_activation_tables
    for idx, (name, funcs) in enumerate(get_activation_tables(nc.m.arch).items()):
        if all(f in funcs for f in want):
            return idx
    raise ValueError(f"no table set with {want}")


def build(stage, interleave=None):
    INTERLEAVE = INTERLEAVE_FLAG[0]
    nc = bass.Bass("TRN2", target_bir_lowering=False, debug=False)
    SET_NLE = _act_set_id(nc, (AF.Exp, AF.Ln))
    SET_SILU = _act_set_id(nc, (AF.Silu,))
    SET_SIG = _act_set_id(nc, (AF.Sigmoid,))
    SET_GELU = _act_set_id(nc, (AF.Gelu_apprx_tanh,))

    x_ln = nc.dram_tensor("x_ln", [8, 128, HID], F32, kind="ExternalInput")
    x_res = nc.dram_tensor("x_res", [16, 64, HID], F32, kind="ExternalInput")
    wqkv_h = nc.dram_tensor("wqkv_h", [128, KC, 768], F32, kind="ExternalInput")
    wgate_h = nc.dram_tensor("wgate_h", [128, KC, 256], F32, kind="ExternalInput")
    wout_h = nc.dram_tensor("wout_h", [128, 2, HID], F32, kind="ExternalInput")
    rmsw_h = nc.dram_tensor("rmsw_h", [128, 2], F32, kind="ExternalInput")
    slope_h = nc.dram_tensor("slope_h", [1, 2], F32, kind="ExternalInput")
    wgp_h = nc.dram_tensor("wgp_h", [1024, NSLC], I32, kind="ExternalInput")
    wup_h = nc.dram_tensor("wup_h", [1024, NSLC], I32, kind="ExternalInput")
    wdp_h = nc.dram_tensor("wdp_h", [1024, HID], I32, kind="ExternalInput")
    sg_h = nc.dram_tensor("sg_h", [1024, NSLC], F32, kind="ExternalInput")
    su_h = nc.dram_tensor("su_h", [1024, NSLC], F32, kind="ExternalInput")
    sd_h = nc.dram_tensor("sd_h", [1024, HID], F32, kind="ExternalInput")
    ident_h = nc.dram_tensor("ident_h", [128, 128], BF16, kind="ExternalInput")
    idx1_h = nc.dram_tensor("idx1_h", [128, BLOCK], F32, kind="ExternalInput")
    dml_h = nc.dram_tensor("dml_h", [128, 2, BLOCK], F32, kind="ExternalInput")
    onesf_h = nc.dram_tensor("onesf_h", [1, 128], F32, kind="ExternalInput")
    onesc_h = nc.dram_tensor("onesc_h", [128, 1], BF16, kind="ExternalInput")

    out_sh = nc.dram_tensor("out_sh", [16, 64, HID], F32, kind="ExternalOutput")

    ag_in = [nc.dram_tensor(f"ag_in{h}", [KC, 128, TCH // 4], BF16) for h in range(4)]
    ag_out = [nc.dram_tensor(f"ag_out{h}", [NCHUNK, KC, 128, TCH // 4], BF16) for h in range(4)]
    ar_in = [nc.dram_tensor(f"ar_in{j}", [1, TCH], F32) for j in range(NCHUNK)]
    ar_out = [nc.dram_tensor(f"ar_out{j}", [1, TCH], F32) for j in range(NCHUNK)]
    wdec_d = nc.dram_tensor("wdec_d", [2, 16, KC, 128, 128], BF16)
    wddec_d = nc.dram_tensor("wddec_d", [KC, 128, HID], BF16)
    inter_d = nc.dram_tensor("inter_d", [16, 16, 128, 512], BF16)
    rs_in = [nc.dram_tensor(f"rs_in{j}", [512, HID], F32) for j in range(16)]
    rs_out = [nc.dram_tensor(f"rs_out{j}", [64, HID], F32) for j in range(16)]

    dbg = {}
    if stage == 1:
        dbg["xn_mine"] = nc.dram_tensor("dbg_xn_mine", [128, KC, 512], BF16, kind="ExternalOutput")
        dbg["xn_c0"] = nc.dram_tensor("dbg_xn_c0", [128, KC, 512], BF16, kind="ExternalOutput")
    if stage == 2:
        dbg["qf"] = nc.dram_tensor("dbg_qf", [128, 2, TCH], BF16, kind="ExternalOutput")
        dbg["kf"] = nc.dram_tensor("dbg_kf", [128, 2, TCH], BF16, kind="ExternalOutput")
        dbg["vt"] = nc.dram_tensor("dbg_vt", [128, 8, 256], BF16, kind="ExternalOutput")
    if stage in (2, 3):
        dbg["o"] = nc.dram_tensor("dbg_o", [128, 2, TTOT], BF16, kind="ExternalOutput")
        dbg["ssq"] = nc.dram_tensor("dbg_ssq", [1, TTOT], F32, kind="ExternalOutput")
    if stage == 4:
        dbg["wdec"] = nc.dram_tensor("dbg_wdec", [2, 16, KC, 128, 128], BF16, kind="ExternalOutput")
        dbg["wddec"] = nc.dram_tensor("dbg_wddec", [KC, 128, HID], BF16, kind="ExternalOutput")
        dbg["inter"] = nc.dram_tensor("dbg_inter", [16, 16, 128, 512], BF16, kind="ExternalOutput")

    es = ExitStack()
    es123 = ExitStack()
    es4 = ExitStack()
    es5 = ExitStack()

    def S(stk, name, shape, dtype):
        return stk.enter_context(nc.sbuf_tensor(name, shape, dtype))

    with es:
        ident = S(es, "ident", [128, 128], BF16)
        idx1 = S(es, "idx1", [128, BLOCK], F32)
        dml = S(es, "dml", [128, 2, BLOCK], F32)
        onesf = S(es, "onesf", [1, 128], F32)
        onesc = S(es, "onesc", [128, 1], BF16)
        srow = S(es, "srow", [1, 2], F32)
        s_all = S(es, "s_all", [128, 2], F32)
        negs = S(es, "negs", [128, 2], F32)
        nb256 = S(es, "nb256", [128, 2], F32)
        qd = S(es, "qd", [128, 2, BLOCK], F32)
        kd = S(es, "kd", [128, 2, BLOCK], F32)
        dd = S(es, "dd", [128, 2, 2, BLOCK], F32)
        bd = S(es, "bd", [128, 2], F32)
        wout = S(es, "wout", [128, 2, HID], BF16)
        rmsw = S(es, "rmsw", [128, 2], F32)
        og_all = S(es, "og_all", [128, 2, TTOT], BF16)
        kv_f = S(es, "kv_f", [128, 2, 128], F32)
        kv_b = S(es, "kv_b", [128, 2, 128], BF16)
        misc3 = S(es, "misc3", [65, TCH], F32)
        es_dec = ExitStack()
        DW = 512
        pk_t = [S(es_dec, "pk_t0", [128, DW], I32)]
        sct_t = [S(es_dec, "sct_t0", [128, DW], F32)]
        dec_u = S(es_dec, "dec_u", [128, DW], I32)
        dec_g = S(es_dec, "dec_g", [128, DW], I32)
        dec_ub = S(es_dec, "dec_ub", [128, DW], BF16)
        dec_a = S(es_dec, "dec_a", [128, DW], BF16)
        dec_b = S(es_dec, "dec_b", [128, DW], BF16)
        dec_p = S(es_dec, "dec_p", [128, DW], BF16)
        dec_w = S(es_dec, "dec_w", [128, DW], BF16)

        PB = [es.enter_context(nc.psum_tensor(f"PB{i}", [128, 1024], BF16)) for i in range(2)]
        PF = [es.enter_context(nc.psum_tensor(f"PF{i}", [128, 512], F32)) for i in range(6)]
        PF2, PF3, PF4, PF5, PF6, PF7 = PF

        DS = Ctr(es.enter_context(nc.semaphore("DS")))
        DG = Ctr(es.enter_context(nc.semaphore("DG")))
        TC = Ctr(es.enter_context(nc.semaphore("TC")))
        VC = Ctr(es.enter_context(nc.semaphore("VC")))
        SC = Ctr(es.enter_context(nc.semaphore("SC")))
        CC = Ctr(es.enter_context(nc.semaphore("CC")))

        bank_tok = {i: None for i in range(8)}

        stacks = []

        with nc.Block() as block:
            @block.sync
            def _(sync):
              try:
                T, V, Sc, G, Y = nc.tensor, nc.vector, nc.scalar, nc.gpsimd, nc.sync

                def w(eng, tok):
                    if tok is not None and tok[1] > 0:
                        eng.wait_ge(tok[0].sem, tok[1])

                def dma(q, ctr, out, in_):
                    ctr.inc(q.dma_start(out=out, in_=in_), 16)
                    return ctr.tok

                def preload(setid):
                    Sc.add_instruction(mybir.InstLoadActFuncSet(
                        name=nc.get_next_instruction_name(),
                        act_func_set_id=setid, ins=[], outs=[]))

                wqkv = S(es123, "wqkv", [128, KC, 768], BF16)
                wgate = S(es123, "wgate", [128, KC, 256], BF16)
                xn_ch = [S(es123, f"xn_ch{i}", [128, KC, 512], BF16) for i in range(2)]
                q_fm2 = [S(es123, f"q_fm{i}", [128, 2, TCH], BF16) for i in range(2)]
                k_fm2 = [S(es123, f"k_fm{i}", [128, 2, TCH], BF16) for i in range(2)]
                v_tm2 = [S(es123, f"v_tm{i}", [128, 8, 256], BF16) for i in range(2)]
                g_ch = [S(es123, f"g_ch{i}", [128, 2, TCH], BF16) for i in range(3)]
                qt_s = S(es123, "qt_s", [128, BLOCK], BF16)
                kt_s = S(es123, "kt_s", [128, BLOCK], BF16)
                ktT = S(es123, "ktT", [128, 2, 128], BF16)
                sc_sb = S(es123, "sc_sb", [128, 2, BLOCK], BF16)
                o2_s = S(es123, "o2_s", [128, BLOCK], BF16)
                es_p1 = ExitStack()
                stacks.append(es_p1)
                xs = S(es_p1, "xs", [128, HID], F32)
                sqb = S(es_p1, "sqb", [128, 1024], BF16)
                stat_all = S(es_p1, "stat_all", [128, 8, 8], F32)
                xn_tm = S(es_p1, "xn_tm", [128, HID], BF16)
                xn_mine = S(es_p1, "xn_mine", [128, KC, 512], BF16)

                preload(SET_NLE)
                t_c = None
                for dst, src in ((ident, ident_h), (idx1, idx1_h), (dml, dml_h),
                                 (onesf, onesf_h), (onesc, onesc_h), (srow, slope_h),
                                 (rmsw, rmsw_h)):
                    t_c = dma(Y, DS, dst[:], src.ap())
                t_wq = dma(G, DG, wqkv[:], wqkv_h.ap())
                t_wg = dma(G, DG, wgate[:], wgate_h.ap())
                t_wo = dma(G, DG, wout[:], wout_h.ap())

                w(T, t_c)
                TC.inc(T.matmul(PF2[:, 0:2], onesf[:], srow[:], start=True, stop=True))
                t = TC.tok
                w(V, t)
                VC.inc(V.tensor_copy(s_all[:], PF2[:, 0:2]))
                w(V, VC.tok)
                VC.inc(V.tensor_scalar(negs[:], s_all[:], -1.0, None, AL.mult))
                VC.inc(V.tensor_scalar(nb256[:], s_all[:], -256.0, None, AL.mult))
                VC.inc(V.memset(kv_f[:], 0.0))
                t_setup_v = VC.tok
                w(Sc, t_setup_v)
                for hh in range(2):
                    SC.inc(Sc.activation(qd[:, hh, :], idx1[:], AF.Exp, scale=negs[:, hh:hh + 1]))
                    SC.inc(Sc.activation(kd[:, hh, :], idx1[:], AF.Exp,
                                         scale=s_all[:, hh:hh + 1], bias=nb256[:, hh:hh + 1]))
                    SC.inc(Sc.activation(dd[:, hh, :, :], dml[:], AF.Exp, scale=negs[:, hh:hh + 1]))
                    SC.inc(Sc.activation(bd[:, hh:hh + 1], s_all[:, hh:hh + 1], AF.Exp, scale=-256.0))
                w(Sc, t_setup_v)
                SC.inc(Sc.copy(kv_b[:], kv_f[:]))
                t_decay = SC.tok
                bank_tok[2] = t_setup_v

                jobs = [(m, kk, qt) for m in range(2) for kk in range(8) for qt in range(4)]
                jobs += [(2, kk, qt) for kk in range(8) for qt in range(4)]
                N_GU_JOBS = 64
                ji = [0]
                pslot = [0]
                slot_tok = [None]
                dec_w_store = [None]

                def emit_decode_one():
                    if ji[0] >= len(jobs):
                        return
                    mat, kk, qt = jobs[ji[0]]
                    ji[0] += 1
                    sl = 0
                    pk, sct = pk_t[0], sct_t[0]
                    src_p = (wgp_h, wup_h, wdp_h)[mat]
                    src_s = (sg_h, su_h, sd_h)[mat]
                    W = DW
                    w(Y, slot_tok[sl])
                    dma(Y, DS, pk[:, :], src_p.ap()[128 * kk:128 * kk + 128, W * qt:W * qt + W])
                    t_pk = dma(Y, DS, sct[:, :], src_s.ap()[128 * kk:128 * kk + 128, W * qt:W * qt + W])
                    w(V, t_pk)
                    for delta in range(2):
                        if delta == 0:
                            VC.inc(V.tensor_single_scalar(dec_u[:, :], pk[:, :], 7, AL.bitwise_and))
                            VC.inc(V.tensor_scalar(dec_g[:, :], pk[:, :], 3, 1,
                                                   AL.logical_shift_right, AL.bitwise_and))
                        else:
                            w(V, VC.tok)
                            VC.inc(V.tensor_scalar(dec_u[:, :], pk[:, :], 4, 7,
                                                   AL.logical_shift_right, AL.bitwise_and))
                            VC.inc(V.tensor_scalar(dec_g[:, :], pk[:, :], 7, 1,
                                                   AL.logical_shift_right, AL.bitwise_and))
                        w(V, VC.tok)
                        VC.inc(V.tensor_copy(dec_ub[:, :], dec_u[:, :]))
                        VC.inc(V.tensor_scalar(dec_p[:, :], dec_g[:, :], -2.0, 1.0, AL.mult, AL.add))
                        w(V, VC.tok)
                        VC.inc(V.tensor_scalar(dec_a[:, :], dec_ub[:, :], 2.0, -8.0, AL.mult, AL.add))
                        w(V, VC.tok)
                        VC.inc(V.scalar_tensor_tensor(dec_b[:, :], dec_ub[:, :], -2.0, dec_a[:, :],
                                                      AL.add, AL.max))
                        w(V, VC.tok)
                        VC.inc(V.scalar_tensor_tensor(dec_a[:, :], dec_ub[:, :], 0.5, dec_b[:, :],
                                                      AL.mult, AL.max))
                        w(V, VC.tok)
                        VC.inc(V.tensor_tensor(dec_b[:, :], dec_a[:, :], dec_p[:, :], AL.mult))
                        w(V, VC.tok)
                        w(V, dec_w_store[0])
                        VC.inc(V.tensor_tensor(dec_w[:, :], dec_b[:, :], sct[:, :], AL.mult))
                        t_w = VC.tok
                        w(G, t_w)
                        q = 2 * kk + delta
                        if mat < 2:
                            dec_w_store[0] = dma(G, DG,
                                wdec_d.ap()[mat, 4 * qt:4 * qt + 4, q].rearrange("nt p n -> p nt n"),
                                dec_w[:, :])
                        else:
                            dec_w_store[0] = dma(G, DG, wddec_d.ap()[q, :, W * qt:W * qt + W], dec_w[:, :])
                    slot_tok[sl] = VC.tok

                xn_cons = [None, None]
                g_cons = [None, None, None]
                attn_cons = [None, None]
                qkv_tok = [None, None]
                ar_tok = [None] * NCHUNK
                ssq_dma_tok = [None]
                gate_guard = {"ln": None, "bc": None}
                sA = {"kvb": t_decay, "qt": None, "kt": None, "ktT": None,
                      "scsb": None, "sq": None, "b3": []}

                def emit_gating(jm):
                    gsl = g_ch[jm % 3]
                    w(Y, ar_tok[jm])
                    w(Y, gate_guard["ln"])
                    t_aro = dma(Y, DS, misc3[64:65, :], ar_out[jm].ap())
                    preload(SET_NLE)
                    w(V, t_aro)
                    VC.inc(V.tensor_scalar(misc3[64:65, :], misc3[64:65, :], 1.0 / HID, RMS_EPS, AL.mult, AL.add))
                    w(Sc, VC.tok)
                    w(Sc, gate_guard["bc"])
                    SC.inc(Sc.activation(misc3[0:1, :], misc3[64:65, :], AF.Ln))
                    gate_guard["ln"] = SC.tok
                    w(Sc, SC.tok)
                    SC.inc(Sc.activation(misc3[0:1, :], misc3[0:1, :], AF.Exp, scale=-0.5))
                    t_rst = SC.tok
                    preload(SET_SILU)
                    for th in range(2):
                        bk = 4 + th
                        w(T, t_rst)
                        w(T, bank_tok[bk])
                        TC.inc(T.matmul(PF[bk - 2][:, :], onesf[:], misc3[0:1, 512 * th:512 * th + 512],
                                        start=True, stop=True))
                        t_bc = TC.tok
                        gate_guard["bc"] = t_bc
                        w(V, t_bc)
                        for hh in range(2):
                            sli = slice(TCH * jm + 512 * th, TCH * jm + 512 * th + 512)
                            VC.inc(V.tensor_tensor(og_all[:, hh, sli], og_all[:, hh, sli],
                                                   gsl[:, hh, 512 * th:512 * th + 512], AL.mult))
                            w(V, VC.tok)
                            VC.inc(V.scalar_tensor_tensor(og_all[:, hh, sli], og_all[:, hh, sli],
                                                          rmsw[:, hh:hh + 1], PF[bk - 2][:, :],
                                                          AL.mult, AL.mult))
                        bank_tok[bk] = VC.tok
                    g_cons[jm % 3] = VC.tok

                rot_banks = [4, 5, 6]

                def gen_proj(j):
                    sl = j % 2
                    qf, kf, vt, gsl = q_fm2[sl], k_fm2[sl], v_tm2[sl], g_ch[j % 3]
                    if j == 0:
                        T.wait_ge(DG.sem, t_wo[1])
                    ri = [0]

                    def nxt_bank():
                        bk = rot_banks[ri[0] % 3]
                        ri[0] += 1
                        return bk, PF[bk - 2]

                    for th in range(2):
                        w(Y, t_ag2[2 * th + 1])
                        w(Y, xn_cons[th])
                        t_xl = None
                        for qt_ in range(2):
                            t_xl = dma(Y, DS, xn_ch[th][:, :, 256 * qt_:256 * qt_ + 256],
                                       ag_out[2 * th + qt_].ap()[j].rearrange("q p t -> p q t"))
                        preload(SET_SILU)
                        for cc in range(4):
                            bk, pf = nxt_bank()
                            w(T, t_xl)
                            w(T, bank_tok[bk])
                            for q in range(KC):
                                mmi = T.matmul(pf[:, :], wqkv[:, q, 128 * cc:128 * cc + 128],
                                               xn_ch[th][:, q, :], start=(q == 0), stop=(q == KC - 1))
                                if q == KC - 1:
                                    TC.inc(mmi)
                            t_mm = TC.tok
                            xn_cons[th] = t_mm
                            hh, isq = cc % 2, cc < 2
                            w(Sc, t_mm)
                            w(Sc, attn_cons[sl])
                            SC.inc(Sc.activation((qf if isq else kf)[:, hh, 512 * th:512 * th + 512],
                                                 pf[:, :], AF.Silu))
                            bank_tok[bk] = SC.tok
                            yield
                        for u in range(4 * th, 4 * th + 4):
                            bk, pf = nxt_bank()
                            w(T, t_xl)
                            w(T, bank_tok[bk])
                            for q in range(KC):
                                mmi = T.matmul(pf[:, 0:256], xn_ch[th][:, q, 128 * (u % 4):128 * (u % 4) + 128],
                                               wqkv[:, q, 512:768], start=(q == 0), stop=(q == KC - 1))
                                if q == KC - 1:
                                    TC.inc(mmi)
                            t_mm = TC.tok
                            xn_cons[th] = t_mm
                            w(Sc, t_mm)
                            w(Sc, attn_cons[sl])
                            SC.inc(Sc.activation(vt[:, u, :], pf[:, 0:256], AF.Silu))
                            bank_tok[bk] = SC.tok
                            yield
                        preload(SET_SIG)
                        for hh in range(2):
                            bk, pf = nxt_bank()
                            w(T, t_xl)
                            w(T, bank_tok[bk])
                            for q in range(KC):
                                mmi = T.matmul(pf[:, :], wgate[:, q, 128 * hh:128 * hh + 128],
                                               xn_ch[th][:, q, :], start=(q == 0), stop=(q == KC - 1))
                                if q == KC - 1:
                                    TC.inc(mmi)
                            t_mm = TC.tok
                            xn_cons[th] = t_mm
                            w(Sc, t_mm)
                            w(Sc, g_cons[j % 3])
                            SC.inc(Sc.activation(gsl[:, hh, 512 * th:512 * th + 512], pf[:, :], AF.Sigmoid))
                            bank_tok[bk] = SC.tok
                            yield
                        if th == 1:
                            qkv_tok[sl] = SC.tok

                def gen_attn(jm):
                    sl = jm % 2
                    qf, kf, vt = q_fm2[sl], k_fm2[sl], v_tm2[sl]
                    if jm == 4:
                        w(V, attn_cons[1])
                        VC.inc(V.memset(kv_f[:], 0.0))
                        w(Sc, VC.tok)
                        SC.inc(Sc.copy(kv_b[:], kv_f[:]))
                        sA["kvb"] = SC.tok
                    for blk in range(4):
                        l0 = BLOCK * blk
                        for hh in range(2):
                            gl = TCH * jm + l0
                            w(V, qkv_tok[sl])
                            w(V, sA["qt"])
                            w(V, sA["kt"])
                            VC.inc(V.tensor_tensor(qt_s[:], qf[:, hh, l0:l0 + BLOCK], qd[:, hh, :], AL.mult))
                            VC.inc(V.tensor_tensor(kt_s[:], kf[:, hh, l0:l0 + BLOCK], kd[:, hh, :], AL.mult))
                            t_qtkt = VC.tok
                            w(T, qkv_tok[sl])
                            w(T, bank_tok[2])
                            for mh in range(2):
                                TC.inc(T.matmul(PF2[:, 256 * mh:256 * mh + 256],
                                                kf[:, hh, l0 + 128 * mh:l0 + 128 * mh + 128],
                                                qf[:, hh, l0:l0 + BLOCK], start=True, stop=True))
                            t_sc_mm = TC.tok
                            yield
                            w(V, t_sc_mm)
                            w(V, sA["scsb"])
                            for mh in range(2):
                                VC.inc(V.tensor_tensor(sc_sb[:, mh, :], PF2[:, 256 * mh:256 * mh + 256],
                                                       dd[:, hh, mh, :], AL.mult))
                            t_scm = VC.tok
                            bank_tok[2] = t_scm
                            w(T, sA["kvb"])
                            w(T, t_qtkt)
                            for tk in sA["b3"]:
                                w(T, tk)
                            TC.inc(T.matmul(PF3[:, 0:256], kv_b[:, hh, :], qt_s[:], start=True, stop=False))
                            sA["qt"] = TC.tok
                            w(T, t_scm)
                            for mh in range(2):
                                TC.inc(T.matmul(PF3[:, 0:256], vt[:, 2 * blk + mh, 128 * hh:128 * hh + 128],
                                                sc_sb[:, mh, :], start=False, stop=(mh == 1)))
                            t_o_mm = TC.tok
                            sA["scsb"] = t_o_mm
                            w(T, bank_tok[1])
                            TC.inc(T.transpose(PB[1][:, 0:128], kt_s[:, 0:128], ident[:]))
                            TC.inc(T.transpose(PB[1][:, 128:256], kt_s[:, 128:256], ident[:]))
                            t_tr = TC.tok
                            sA["kt"] = t_tr
                            w(Sc, t_tr)
                            w(Sc, sA["ktT"])
                            SC.inc(Sc.copy(ktT[:, 0, :], PB[1][:, 0:128]))
                            SC.inc(Sc.copy(ktT[:, 1, :], PB[1][:, 128:256]))
                            t_ktT = SC.tok
                            bank_tok[1] = t_ktT
                            w(T, t_ktT)
                            for mh in range(2):
                                TC.inc(T.matmul(PF3[:, 256:384], ktT[:, mh, :],
                                                vt[:, 2 * blk + mh, 128 * hh:128 * hh + 128],
                                                start=(mh == 0), stop=(mh == 1)))
                            t_a_mm = TC.tok
                            sA["ktT"] = t_a_mm
                            w(Sc, t_a_mm)
                            SC.inc(Sc.copy(og_all[:, hh, gl:gl + BLOCK], PF3[:, 0:256]))
                            t_oev = SC.tok
                            w(V, t_a_mm)
                            w(V, t_oev)
                            w(V, sA["kvb"])
                            VC.inc(V.scalar_tensor_tensor(kv_f[:, hh, :], kv_f[:, hh, :], bd[:, hh:hh + 1],
                                                          PF3[:, 256:384], AL.mult, AL.add))
                            t_kvf = VC.tok
                            sA["b3"] = [t_oev, t_kvf]
                            w(Sc, t_kvf)
                            SC.inc(Sc.copy(kv_b[:, hh, :], kv_f[:, hh, :]))
                            sA["kvb"] = SC.tok
                            w(V, t_oev)
                            w(V, sA["sq"])
                            VC.inc(V.tensor_tensor(o2_s[:], og_all[:, hh, gl:gl + BLOCK],
                                                   og_all[:, hh, gl:gl + BLOCK], AL.mult))
                            t_o2 = VC.tok
                            w(T, t_o2)
                            if blk % 2 == 0 and hh == 0:
                                w(T, bank_tok[7])
                            TC.inc(T.matmul(PF7[0:1, 256 * (blk % 2):256 * (blk % 2) + 256],
                                            onesc[:], o2_s[:], start=(hh == 0), stop=(hh == 1)))
                            sA["sq"] = TC.tok
                            attn_cons[sl] = TC.tok
                            yield
                        if blk % 2 == 1:
                            w(V, sA["sq"])
                            w(V, ssq_dma_tok[0])
                            VC.inc(V.tensor_copy(misc3[32:33, 512 * (blk // 2):512 * (blk // 2) + 512],
                                                 PF7[0:1, :]))
                            bank_tok[7] = VC.tok
                    t_sev = VC.tok
                    w(G, t_sev)
                    t_ast = dma(G, DG, ar_in[jm].ap(), misc3[32:33, :])
                    ssq_dma_tok[0] = t_ast
                    w(G, t_ast)
                    CC.inc(G.collective_compute(
                        "AllReduce", AL.add, replica_groups=[list(range(8))],
                        ins=[ar_in[jm].ap().opt()], outs=[ar_out[jm].ap().opt()]))
                    ar_tok[jm] = CC.tok

                gp_pre = [gen_proj(0)] if stage != 1 else None


                xs_cons = []
                t_trp = None
                t_agst_prev = None
                t_ag2 = [None] * 4
                for u in range(8):
                    for tk in xs_cons:
                        w(Y, tk)
                    t_x = dma(Y, DS, xs[:], x_ln.ap()[u])
                    w(V, t_x)
                    VC.inc(V.reduce_sum(stat_all[:, u, 0:1], xs[:], axis=AX))
                    t_sum = VC.tok
                    w(Sc, t_x)
                    SC.inc(Sc.activation(sqb[:], xs[:, 0:1024], AF.Square, accum_out=stat_all[:, u, 1:2]))
                    w(Sc, SC.tok)
                    SC.inc(Sc.activation(sqb[:], xs[:, 1024:2048], AF.Square, accum_out=stat_all[:, u, 7:8]))
                    xs_cons = [t_sum, SC.tok]
                t_ssq_all = SC.tok
                w(V, t_ssq_all)
                for u in range(8):
                    VC.inc(V.tensor_tensor(stat_all[:, u, 1:2], stat_all[:, u, 1:2],
                                           stat_all[:, u, 7:8], AL.add))
                for u in range(8):
                    w(V, VC.tok)
                    VC.inc(V.tensor_scalar(stat_all[:, u, 2:3], stat_all[:, u, 0:1], 1.0 / HID, None, AL.mult))
                w(V, t_ssq_all)
                for u in range(8):
                    VC.inc(V.tensor_scalar(stat_all[:, u, 4:5], stat_all[:, u, 1:2], 1.0 / HID, LN_EPS,
                                           AL.mult, AL.add))
                w(V, VC.tok)
                for u in range(8):
                    VC.inc(V.tensor_tensor(stat_all[:, u, 3:4], stat_all[:, u, 2:3], stat_all[:, u, 2:3], AL.mult))
                w(V, VC.tok)
                for u in range(8):
                    VC.inc(V.tensor_tensor(stat_all[:, u, 4:5], stat_all[:, u, 4:5], stat_all[:, u, 3:4], AL.subtract))
                t_var = VC.tok
                w(Sc, t_var)
                for u in range(8):
                    SC.inc(Sc.activation(stat_all[:, u, 5:6], stat_all[:, u, 4:5], AF.Ln))
                w(Sc, SC.tok)
                for u in range(8):
                    SC.inc(Sc.activation(stat_all[:, u, 6:7], stat_all[:, u, 5:6], AF.Exp, scale=-0.5))
                t_rstd = SC.tok
                for u in range(8):
                    for tk in xs_cons:
                        w(Y, tk)
                    t_x = dma(Y, DS, xs[:], x_ln.ap()[u])
                    w(V, t_x)
                    w(V, t_rstd)
                    w(V, t_trp)
                    VC.inc(V.tensor_scalar(xn_tm[:], xs[:], stat_all[:, u, 2:3], stat_all[:, u, 6:7],
                                           AL.subtract, AL.mult))
                    t_norm = VC.tok
                    xs_cons = [t_norm]
                    w(T, t_norm)
                    for q in range(KC):
                        b = q % 2
                        w(T, bank_tok[b])
                        base = 256 * (q // 2) + (q % 2)
                        TC.inc(T.transpose(PB[b][:, 0:128], xn_tm[:, base:base + 255 + 1 - (q % 2):2], ident[:]))
                        t_trp = TC.tok
                        w(Sc, TC.tok)
                        if u % 4 == 0 and q == 0:
                            w(Sc, t_agst_prev)
                        SC.inc(Sc.copy(xn_mine[:, q, 128 * (u % 4):128 * (u % 4) + 128], PB[b][:, 0:128]))
                        bank_tok[b] = SC.tok
                    if u % 2 == 1:
                        hf_ = u // 2
                        w(G, SC.tok)
                        t_agst = dma(G, DG, ag_in[hf_].ap().rearrange("q p t -> p q t"),
                                     xn_mine[:, :, 256 * (hf_ % 2):256 * (hf_ % 2) + 256])
                        t_agst_prev = t_agst
                        w(G, t_agst)
                        CC.inc(G.collective_compute(
                            "AllGather", AL.bypass, replica_groups=[list(range(8))],
                            ins=[ag_in[hf_].ap().opt()], outs=[ag_out[hf_].ap().opt()]))
                        t_ag2[hf_] = CC.tok
                    if u >= 4 and gp_pre is not None:
                        next(gp_pre[0], None)
                        next(gp_pre[0], None)
                t_ag = t_ag2[3]

                if stage == 1:
                    dma(G, DG, dbg["xn_mine"].ap(), xn_mine[:])
                es_p1.close()
                if stage == 1:
                    w(Y, t_ag)
                    for qt_ in range(2):
                        t_l = dma(Y, DS, xn_ch[0][:, :, 256 * qt_:256 * qt_ + 256],
                                  ag_out[qt_].ap()[0].rearrange("q p t -> p q t"))
                    w(G, t_l)
                    dma(G, DG, dbg["xn_c0"].ap(), xn_ch[0][:])
                    w(G, DG.tok); w(Y, DS.tok); w(Sc, SC.tok); w(V, VC.tok); w(T, TC.tok)
                    return

                ga = None
                for j in range(NCHUNK):
                    gp = gp_pre[0] if j == 0 else gen_proj(j)
                    if ga is None:
                        for _ in gp:
                            pass
                    elif INTERLEAVE:
                        kq = 0
                        for _ in ga:
                            next(gp, None)
                            if kq % 4 == 3:
                                next(gp, None)
                            if kq == 5 and stage != 2 and j >= 2:
                                emit_gating(j - 2)
                            if kq % 2 == 1:
                                emit_decode_one()
                            kq += 1
                        for _ in gp:
                            pass
                    else:
                        kq = 0
                        for _ in ga:
                            if kq % 2 == 1:
                                emit_decode_one()
                            kq += 1
                        for _ in gp:
                            pass
                        if stage != 2:
                            emit_gating(j - 1)
                    if stage == 2 and j == 0:
                        w(G, qkv_tok[0])
                        dma(G, DG, dbg["qf"].ap(), q_fm2[0][:])
                        dma(G, DG, dbg["kf"].ap(), k_fm2[0][:])
                        dma(G, DG, dbg["vt"].ap(), v_tm2[0][:])
                    ga = gen_attn(j)
                kq = 0
                for _ in ga:
                    if kq == 5 and stage != 2:
                        emit_gating(NCHUNK - 2)
                    if kq % 2 == 1:
                        emit_decode_one()
                    kq += 1
                if stage != 2:
                    emit_gating(NCHUNK - 1)

                if stage in (2, 3):
                    w(G, VC.tok)
                    w(G, SC.tok)
                    dma(G, DG, dbg["o"].ap(), og_all[:])
                    for jq in range(NCHUNK):
                        dma(G, DG, dbg["ssq"].ap()[0:1, TCH * jq:TCH * jq + TCH], ar_in[jq].ap())
                    w(G, DG.tok); w(Y, DS.tok); w(Sc, SC.tok); w(V, VC.tok); w(T, TC.tok)
                    return

                while ji[0] < N_GU_JOBS:
                    emit_decode_one()
                t_gu_dec = DG.tok

                es123.close()
                xn_tc = [S(es4, f"xn_tc{i}", [128, KC, 512], BF16) for i in range(2)]
                wl_g = [S(es4, f"wl_g{i}", [128, KC, 4, 128], BF16) for i in range(2)]
                wl_u = [S(es4, f"wl_u{i}", [128, KC, 4, 128], BF16) for i in range(2)]
                gel_s = S(es4, "gel_s", [128, 4, 512], BF16)
                up_s = S(es4, "up_s", [128, 512], BF16)
                int_s = [S(es4, f"int_s{i}", [128, 4, 512], BF16) for i in range(2)]

                t_bar = (TC.tok, VC.tok, SC.tok)
                for tok in t_bar:
                    w(Y, tok)

                preload(SET_GELU)
                xtc_cons = [None, None]
                wl_cons = [None, None]
                ist_tok = [None, None]
                gel_cons = [None] * 4
                for ntg in range(4):
                    ws = ntg % 2
                    w(Y, wl_cons[ws])
                    if ntg == 0:
                        w(Y, t_gu_dec)
                    for nt4 in range(4):
                        t_wgl = dma(Y, DS, wl_g[ws][:, :, nt4, :],
                                    wdec_d.ap()[0, 4 * ntg + nt4].rearrange("q p n -> p q n"))
                        t_wul = dma(Y, DS, wl_u[ws][:, :, nt4, :],
                                    wdec_d.ap()[1, 4 * ntg + nt4].rearrange("q p n -> p q n"))
                    for tc in range(16):
                        ts_ = tc % 2
                        jj, th = tc // 2, tc % 2
                        w(Y, xtc_cons[ts_])
                        for qt_ in range(2):
                            t_x = dma(Y, DS, xn_tc[ts_][:, :, 256 * qt_:256 * qt_ + 256],
                                      ag_out[2 * th + qt_].ap()[jj].rearrange("q p t -> p q t"))
                        isl = tc % 2
                        w(T, t_wul)
                        w(T, t_x)
                        for nt4 in range(4):
                            bk = 4 + (nt4 % 2) * 2
                            pf = PF[2 + (nt4 % 2) * 2]
                            w(T, bank_tok[bk])
                            for q in range(KC):
                                mmi = T.matmul(pf[:, :], wl_g[ws][:, q, nt4, :],
                                               xn_tc[ts_][:, q, :], start=(q == 0), stop=(q == KC - 1))
                                if q == KC - 1:
                                    TC.inc(mmi)
                            t_mm = TC.tok
                            w(Sc, t_mm)
                            w(Sc, gel_cons[nt4])
                            SC.inc(Sc.activation(gel_s[:, nt4, :], pf[:, :], AF.Gelu_apprx_tanh))
                            bank_tok[bk] = SC.tok
                            bku = 5 + (nt4 % 2) * 2
                            pfu = PF[3 + (nt4 % 2) * 2]
                            w(T, bank_tok[bku])
                            for q in range(KC):
                                mmi = T.matmul(pfu[:, :], wl_u[ws][:, q, nt4, :],
                                               xn_tc[ts_][:, q, :], start=(q == 0), stop=(q == KC - 1))
                                if q == KC - 1:
                                    TC.inc(mmi)
                            t_mmu = TC.tok
                            xtc_cons[ts_] = t_mmu
                            wl_cons[ws] = t_mmu
                            w(V, t_mmu)
                            w(V, ist_tok[isl])
                            VC.inc(V.tensor_copy(up_s[:, :], pfu[:, :]))
                            w(V, VC.tok)
                            w(V, bank_tok[bk])
                            VC.inc(V.tensor_tensor(int_s[isl][:, nt4, :], gel_s[:, nt4, :], up_s[:, :], AL.mult))
                            bank_tok[bku] = VC.tok
                            gel_cons[nt4] = VC.tok
                        t_int = VC.tok
                        w(G, t_int)
                        ist_tok[isl] = dma(G, DG,
                                           inter_d.ap()[4 * ntg:4 * ntg + 4, tc].rearrange("nt p t -> p nt t"),
                                           int_s[isl][:, :, :])
                        if tc % 2 == 0:
                            emit_decode_one()
                while ji[0] < len(jobs):
                    emit_decode_one()

                if stage == 4:
                    w(G, DG.tok)
                    dma(G, DG, dbg["wdec"].ap(), wdec_d.ap())
                    dma(G, DG, dbg["wddec"].ap(), wddec_d.ap())
                    dma(G, DG, dbg["inter"].ap(), inter_d.ap())
                    w(G, DG.tok); w(Y, DS.tok); w(Sc, SC.tok); w(V, VC.tok); w(T, TC.tok)
                    return

                es4.close()
                es_dec.close()
                wd_res = S(es5, "wd_res", [128, KC, HID], BF16)
                int_l = [S(es5, f"int_l{i}", [128, KC, 512], BF16) for i in range(2)]
                rs_st = [S(es5, f"rs_st{i}", [128, HID], F32) for i in range(2)]
                xr_s = S(es5, "xr_s", [64, HID], F32)
                ro_s = S(es5, "ro_s", [64, HID], F32)
                ou_s = S(es5, "ou_s", [64, HID], F32)

                t_bar2 = (TC.tok, VC.tok, SC.tok)
                for tok in t_bar2:
                    w(Y, tok)
                w(Y, DG.tok)
                t_wd = dma(Y, DS, wd_res[:], wddec_d.ap().rearrange("q p h -> p q h"))

                il_cons = [None, None]
                rs_dma = [None, None]
                rs_tok = [None] * 16
                p6_state = {"add": None, "ost": None}

                def p6(jj):
                    w(Y, rs_tok[jj])
                    w(Y, p6_state["add"])
                    t_ro = dma(Y, DS, ro_s[:], rs_out[jj].ap())
                    t_xr = dma(Y, DS, xr_s[:], x_res.ap()[jj])
                    w(V, t_xr)
                    w(V, p6_state["ost"])
                    VC.inc(V.tensor_tensor(ou_s[:], ro_s[:], xr_s[:], AL.add))
                    p6_state["add"] = VC.tok
                    w(G, p6_state["add"])
                    p6_state["ost"] = dma(G, DG, out_sh.ap()[jj], ou_s[:])

                for tc in range(16):
                    ts_ = tc % 2
                    w(Y, il_cons[ts_])
                    t_il = dma(Y, DS, int_l[ts_][:], inter_d.ap()[:, tc].rearrange("nt p t -> p nt t"))
                    for s in range(4):
                        tt = 4 * tc + s
                        rsl = tt % 2
                        w(T, t_il)
                        w(T, t_wd)
                        pfm = [PF[(4 * tt + h4) % 6] for h4 in range(4)]
                        for h4 in range(4):
                            w(T, bank_tok[2 + (4 * tt + h4) % 6])
                        for q in range(KC):
                            for h4 in range(4):
                                T.matmul(pfm[h4][:, :], int_l[ts_][:, q, 128 * s:128 * s + 128],
                                         wd_res[:, q, 512 * h4:512 * h4 + 512],
                                         start=(q == 0), stop=False)
                        for jc in range(2):
                            for h4 in range(4):
                                mmi = T.matmul(pfm[h4][:, :], og_all[:, jc, 128 * tt:128 * tt + 128],
                                               wout[:, jc, 512 * h4:512 * h4 + 512],
                                               start=False, stop=(jc == 1))
                                if jc == 1:
                                    TC.inc(mmi)
                        t_mm = TC.tok
                        il_cons[ts_] = t_mm
                        w(Sc, t_mm)
                        w(Sc, rs_dma[rsl])
                        SC.inc(Sc.copy(rs_st[rsl][:, 0:512], pfm[0][:, :]))
                        SC.inc(Sc.copy(rs_st[rsl][:, 512:1024], pfm[1][:, :]))
                        bank_tok[2 + (4 * tt) % 6] = SC.tok
                        bank_tok[2 + (4 * tt + 1) % 6] = SC.tok
                        t_ev_s = SC.tok
                        w(V, t_mm)
                        w(V, rs_dma[rsl])
                        VC.inc(V.tensor_copy(rs_st[rsl][:, 1024:1536], pfm[2][:, :]))
                        VC.inc(V.tensor_copy(rs_st[rsl][:, 1536:2048], pfm[3][:, :]))
                        bank_tok[2 + (4 * tt + 2) % 6] = VC.tok
                        bank_tok[2 + (4 * tt + 3) % 6] = VC.tok
                        t_ev_v = VC.tok
                        w(G, t_ev_s)
                        w(G, t_ev_v)
                        jj, row = tt // 4, tt % 4
                        rs_dma[rsl] = dma(G, DG, rs_in[jj].ap()[128 * row:128 * row + 128, :],
                                          rs_st[rsl][:, :])
                    w(G, rs_dma[0])
                    w(G, rs_dma[1])
                    CC.inc(G.collective_compute(
                        "ReduceScatter", AL.add, replica_groups=[list(range(8))],
                        ins=[rs_in[tc].ap().opt()], outs=[rs_out[tc].ap().opt()]))
                    rs_tok[tc] = CC.tok
                    if tc >= 1:
                        p6(tc - 1)
                p6(15)
                w(G, DG.tok)
                w(Y, DS.tok)
                w(Sc, SC.tok)
                w(V, VC.tok)
                w(T, TC.tok)
              finally:
                for st in stacks + [es5, es4, es123, es_dec]:
                    try:
                        st.close()
                    except Exception:
                        pass
    return nc, dbg


def _host_prep(inputs):
    x = np.ascontiguousarray(np.asarray(inputs["L0_attn_hidden_states"], dtype=np.float32).reshape(TTOT, HID))
    slope = np.asarray(inputs["L0_attn_slope_rate"], dtype=np.float32).reshape(H)
    w_qkv = np.asarray(inputs["L0_attn_w_qkv"], dtype=np.float32)
    w_gate = np.asarray(inputs["L0_attn_w_gate"], dtype=np.float32)
    w_out = np.asarray(inputs["L0_attn_w_out"], dtype=np.float32)
    rms_w = np.asarray(inputs["L0_attn_rms_weight"], dtype=np.float32)
    wgp = np.asarray(inputs["L0_ffn_gate_W_packed"], dtype=np.int32)
    sgs = np.asarray(inputs["L0_ffn_gate_scales"], dtype=np.float32)
    wup = np.asarray(inputs["L0_ffn_up_W_packed"], dtype=np.int32)
    sus = np.asarray(inputs["L0_ffn_up_scales"], dtype=np.float32)
    wdp = np.asarray(inputs["L0_ffn_down_W_packed"], dtype=np.int32)
    sds = np.asarray(inputs["L0_ffn_down_scales"], dtype=np.float32)

    wqkv_p = w_qkv[PERM]
    wgate_p = w_gate[PERM]
    wgpT = np.ascontiguousarray(wgp.T)
    wupT = np.ascontiguousarray(wup.T)
    wdpT = np.ascontiguousarray(wdp.T)
    sgsT = np.ascontiguousarray(sgs.T)
    susT = np.ascontiguousarray(sus.T)
    sdsT = np.ascontiguousarray(sds.T)

    ident = np.eye(128, dtype=ml_dtypes.bfloat16)
    idx1 = np.ascontiguousarray(np.tile(np.arange(1, BLOCK + 1, dtype=np.float32), (128, 1)))
    dmlt = np.empty((128, 2, BLOCK), dtype=np.float32)
    for qq in range(2):
        m = 128 * qq + np.arange(128)[:, None]
        l = np.arange(BLOCK)[None, :]
        dmlt[:, qq, :] = np.where(l >= m, (l - m).astype(np.float32), BIGM)
    onesf = np.ones((1, 128), dtype=np.float32)
    onesc = np.ones((128, 1), dtype=ml_dtypes.bfloat16)

    maps = []
    xr4 = x.reshape(16, 8, 64, HID)
    for c in range(8):
        cols = np.r_[256 * c:256 * c + 256, 2048 + 256 * c:2048 + 256 * c + 256,
                     4096 + 256 * c:4096 + 256 * c + 256]
        nsl = 2048 * c + PERM
        m = dict(
            x_ln=np.ascontiguousarray(x[1024 * c:1024 * c + 1024].reshape(8, 128, HID)),
            x_res=np.ascontiguousarray(xr4[:, c]),
            wqkv_h=np.ascontiguousarray(wqkv_p[:, cols].reshape(KC, 128, 768).transpose(1, 0, 2)),
            wgate_h=np.ascontiguousarray(
                wgate_p[:, 256 * c:256 * c + 256].reshape(KC, 128, 256).transpose(1, 0, 2)),
            wout_h=np.ascontiguousarray(w_out[256 * c:256 * c + 256].reshape(2, 128, HID).transpose(1, 0, 2)),
            rmsw_h=np.ascontiguousarray(rms_w[256 * c:256 * c + 256].reshape(2, 128).T),
            slope_h=np.ascontiguousarray(slope[[2 * c, 2 * c + 1]].reshape(1, 2)),
            wgp_h=np.ascontiguousarray(wgpT[:, nsl]),
            wup_h=np.ascontiguousarray(wupT[:, nsl]),
            wdp_h=np.ascontiguousarray(wdpT[1024 * c:1024 * c + 1024]),
            sg_h=np.ascontiguousarray(np.repeat(sgsT[:, nsl], 16, axis=0)),
            su_h=np.ascontiguousarray(np.repeat(susT[:, nsl], 16, axis=0)),
            sd_h=np.ascontiguousarray(np.repeat(sdsT[64 * c:64 * c + 64], 16, axis=0)),
            ident_h=ident, idx1_h=idx1, dml_h=dmlt, onesf_h=onesf, onesc_h=onesc,
        )
        maps.append(m)
    return maps


def kernel(**inputs):
    maps = _host_prep(inputs)
    nc, dbg = build(STAGE)
    res = run_bass_kernel_spmd(nc, maps, core_ids=list(range(8)))
    if STAGE < 5:
        return res
    out = np.empty((TTOT, HID), dtype=np.float32)
    for c in range(8):
        sh = res.results[c]["out_sh"]
        for h in range(16):
            out[512 * h + 64 * c:512 * h + 64 * c + 64] = sh[h]
    return out.reshape(B, N, HID)
